# version 12
# baseline (speedup 1.0000x reference)
import math
import numpy as np
import concourse.bass as bass
import concourse.mybir as mybir
from concourse.bass_utils import run_bass_kernel_spmd

F32 = mybir.dt.float32
BF16 = mybir.dt.bfloat16
AF = mybir.ActivationFunctionType
ALU = mybir.AluOpType
AX = mybir.AxisListType

ISQ128 = float(1.0 / math.sqrt(128.0))


class Cfg:
    def __init__(self, S=2048, NH=16, FH=8192, B=2, NCORES=8):
        self.S = S
        self.NH = NH
        self.D = NH * 128
        self.FH = FH
        self.B = B
        self.NCORES = NCORES
        self.PAIRS = NH // 2
        self.DC = self.D // 128
        self.A = FH // 128
        self.QC = max(1, S // 512)
        self.TT = S // 128
        self.KPQ = (S // self.QC) // 128
        self.NSUP = 4 if (B * self.TT) % 4 == 0 else 1


class Sem:
    __slots__ = ("h", "n", "name")

    def __init__(self, nc, name):
        self.h = nc.alloc_semaphore(name)
        self.n = 0
        self.name = name

    def tok(self):
        return (self, self.n)


class Prog:
    ENGS = ("t", "v", "s", "g", "y")

    def __init__(self, nc):
        self.nc = nc
        self.q = {e: [] for e in self.ENGS}
        self.label = ""
        self.hwm = {e: {} for e in self.ENGS}

    def emit(self, e, fn, waits=(), incs=()):
        ws = []
        hw = self.hwm[e]

        def _flat(items):
            for w in items:
                if w is None:
                    continue
                if isinstance(w, list):
                    _flat(w)
                elif w[1] > hw.get(w[0], 0):
                    hw[w[0]] = w[1]
                    ws.append(w)
        _flat(waits)
        for s, k in incs:
            s.n += k
        self.q[e].append((fn, tuple(ws), tuple(incs), self.label))

    def run(self):
        block_map = {"t": "tensor", "v": "vector", "s": "scalar",
                     "g": "gpsimd", "y": "sync"}
        with self.nc.Block() as block:
            for e in self.ENGS:
                ops = self.q[e]

                def make_runner(ops):
                    def _(eng):
                        for fn, ws, incs, lbl in ops:
                            for s, v in ws:
                                eng.wait_ge(s.h, v)
                            inst = fn(eng)
                            if inst is not None and lbl:
                                inst.annotate(lbl)
                            for s, k in incs:
                                inst = inst.then_inc(s.h, k)
                    return _
                getattr(block, block_map[e])(make_runner(ops))


def build(cfg: Cfg):
    nc = bass.Bass("TRN2", target_bir_lowering=False)
    S, D, NH, FH, B = cfg.S, cfg.D, cfg.NH, cfg.FH, cfg.B
    DC, A, QC, TT, KPQ = cfg.DC, cfg.A, cfg.QC, cfg.TT, cfg.KPQ
    PAIRS, NSUP = cfg.PAIRS, cfg.NSUP
    NCH = S // 512 if S >= 512 else 1
    CW = min(512, S)

    d_xT = nc.dram_tensor("xT", [B, D, S], F32, kind="ExternalInput").ap()
    d_xn = nc.dram_tensor("x_nat", [B, S, D], F32, kind="ExternalInput").ap()
    d_wq = nc.dram_tensor("wqT_p", [D, D], F32, kind="ExternalInput").ap()
    d_wk = nc.dram_tensor("wkT_p", [D, D], F32, kind="ExternalInput").ap()
    d_wv = nc.dram_tensor("wvT", [D, D], F32, kind="ExternalInput").ap()
    d_wo = nc.dram_tensor("woT", [D, D], F32, kind="ExternalInput").ap()
    d_w1 = nc.dram_tensor("w1T_p", [D, FH], F32, kind="ExternalInput").ap()
    d_w3 = nc.dram_tensor("w3T_p", [D, FH], F32, kind="ExternalInput").ap()
    d_w2 = nc.dram_tensor("w2T", [FH, D], F32, kind="ExternalInput").ap()
    d_cos = nc.dram_tensor("cosd", [128, S], F32, kind="ExternalInput").ap()
    d_sin = nc.dram_tensor("sind", [128, S], F32, kind="ExternalInput").ap()
    d_h128 = nc.dram_tensor("h128c", [128, 128], F32, kind="ExternalInput").ap()
    d_hA = nc.dram_tensor("hAc", [128, min(A, 64)], F32,
                          kind="ExternalInput").ap()
    d_id = nc.dram_tensor("identc", [128, 128], F32, kind="ExternalInput").ap()
    d_tri = nc.dram_tensor("tric", [128, 128], F32, kind="ExternalInput").ap()
    d_y = nc.dram_tensor("y", [B, S, D], F32, kind="ExternalOutput").ap()
    d_w2pp = nc.dram_tensor("w2pp_s", [D // 512, FH // (128 * 16), 128,
                                       16 * 512], BF16, kind="Internal").ap()
    d_hid = nc.dram_tensor("hid_s", [128, A, B * S], BF16,
                           kind="Internal").ap()
    d_y1T = nc.dram_tensor("y1T_s", [DC, 128, B * S], BF16,
                           kind="Internal").ap()
    d_y1f = nc.dram_tensor("y1f_s", [B * S, D], F32, kind="Internal").ap()
    d_oT = nc.dram_tensor("oT_s", [B, NH, 128, S], BF16, kind="Internal").ap()

    SBF = {}
    OFFS = {}
    _off = [16640]

    def At(name, shape, dtype, off=None):
        if off is None:
            off = _off[0]
            per = int(np.prod(shape[1:])) * (2 if dtype == BF16 else 4)
            _off[0] = off + ((per + 31) // 32) * 32
        t = nc.alloc_sbuf_tensor_at(name, shape, dtype, offset=off)
        SBF[name] = t[:]
        OFFS[name] = off
        return t[:]

    def size_of(shape, dtype):
        return int(np.prod(shape[1:])) * (2 if dtype == BF16 else 4)

    consts = At("consts", [128, 16], F32)
    onesf = At("onesf", [1, 128], F32)
    onesb = At("onesb", [128, 128], BF16)
    h128b = At("h128b", [128, 128], BF16)
    hAb = At("hAb", [128, min(A, 64)], BF16)
    identb = At("identf", [128, 128], F32)
    trib = At("trib", [128, 128], BF16)
    lncol = At("lncol", [128, 8], F32)
    nmscal = At("nmscal", [1, 8], F32)
    cosb = At("cosb", [128, S], BF16)
    sinb = At("sinb", [128, S], BF16)
    PBASE = ((_off[0] + 63) // 64) * 64

    P1 = PBASE
    P1_SZ = size_of([128, DC * S], BF16)
    P2 = P1 + P1_SZ
    o = P2
    def nxt(shape, dtype):
        nonlocal o
        off = o
        o += ((size_of(shape, dtype) + 31) // 32) * 32
        return off
    xbT = At("xbT", [128, DC * S], BF16, P1)
    qe = At("qe", [128, S], BF16, nxt([128, S], BF16))
    qo = At("qo", [128, S], BF16, nxt([128, S], BF16))
    ke = At("ke", [128, S], BF16, nxt([128, S], BF16))
    ko = At("ko", [128, S], BF16, nxt([128, S], BF16))
    vpair = At("vpair", [128, TT * 256], BF16, nxt([128, TT * 256], BF16))
    wqblk = [At(f"wqblk{i}", [128, DC * 256], BF16,
                nxt([128, DC * 256], BF16)) for i in range(2)]
    wkblk = [At(f"wkblk{i}", [128, DC * 256], BF16,
                nxt([128, DC * 256], BF16)) for i in range(2)]
    wvblk = [At(f"wvblk{i}", [128, DC * 256], BF16,
                nxt([128, DC * 256], BF16)) for i in range(2)]
    ptile = [At(f"ptile{i}", [128, CW], BF16, nxt([128, CW], BF16))
             for i in range(4)]
    rtmp = [At(f"rtmp{i}", [128, CW], F32, nxt([128, CW], F32))
            for i in range(2)]
    bcsb = [At(f"bcsb{i}", [128, CW], F32, nxt([128, CW], F32))
            for i in range(2)]
    otstage = [At(f"otstage{i}", [128, S], BF16, nxt([128, S], BF16))
               for i in range(2)]
    FP = 8
    FD = 512
    w2fin = [At(f"w2fin{i}", [A, FP * FD], BF16, nxt([A, FP * FD], BF16))
             for i in range(2)]
    w2fst = [At(f"w2fst{i}", [A, FP * FD], BF16, nxt([A, FP * FD], BF16))
             for i in range(2)]
    A_END = o
    o = P2
    wobuf = At("wobuf", [128, DC * D], BF16, nxt([128, DC * D], BF16))
    xnat = [At(f"xnat{i}", [128, D], F32, nxt([128, D], F32))
            for i in range(2)]
    ottile = [At(f"ottile{i}", [128, NH * 128], BF16,
                 nxt([128, NH * 128], BF16)) for i in range(2)]
    zsb = At("zsb", [128, D], F32, nxt([128, D], F32))
    sqf = At("sqf", [128, D], F32, nxt([128, D], F32))
    y1f = [At(f"y1f{i}", [128, D], F32, nxt([128, D], F32))
           for i in range(2)]
    y1Tst = [At(f"y1Tst{i}", [128, DC * 128], BF16,
                nxt([128, DC * 128], BF16)) for i in range(2)]
    W_END = o
    y1T = At("y1T", [128, DC * B * S], BF16, P1)
    o = P1 + size_of([128, DC * B * S], BF16)
    w1blk = [At(f"w1blk{i}", [128, DC * 128], BF16,
                nxt([128, DC * 128], BF16)) for i in range(2)]
    w3blk = [At(f"w3blk{i}", [128, DC * 128], BF16,
                nxt([128, DC * 128], BF16)) for i in range(2)]
    silub = [At(f"silub{i}", [128, 512], BF16, nxt([128, 512], BF16))
             for i in range(2)]
    hidb = [At(f"hidb{i}", [128, 512], BF16, nxt([128, 512], BF16))
            for i in range(2)]
    hid2b = [At(f"hid2b{i}", [128, 512], BF16, nxt([128, 512], BF16))
             for i in range(2)]
    U_END = o
    o = P1
    hsup = At("hsup", [128, NSUP * A * 128], BF16,
              nxt([128, NSUP * A * 128], BF16))
    z2sb = At("z2sb", [128, NSUP * D], F32, nxt([128, NSUP * D], F32))
    WQ = A // 4 if A >= 4 else A
    w2blk = [At(f"w2blk{i}", [128, WQ * 512], BF16,
                nxt([128, WQ * 512], BF16)) for i in range(2)]
    y1in = [At(f"y1in{i}", [128, D], F32, nxt([128, D], F32))
            for i in range(NSUP)]
    outsb = [At(f"outsb{i}", [128, D], F32, nxt([128, D], F32))
             for i in range(2)]
    sqfD = At("sqfD", [128, D], F32, nxt([128, D], F32))
    D_END = o

    ps = [nc.alloc_psum_tensor(f"ps{i}", [128, 512], F32)[:]
          for i in range(8)]

    CVALS = dict(isq=ISQ128, invD=1.0 / D, eps=1e-5, nhalf=-0.5,
                 sq2i=0.70710678118, zero=0.0, one=1.0)
    for i, (k, v) in enumerate(CVALS.items()):
        nc.const_aps.register(F32, float(v), consts[:, i:i + 1])

    P = Prog(nc)
    tq = Sem(nc, "tq")
    vq = Sem(nc, "vq")
    sq = Sem(nc, "sq")
    cs = Sem(nc, "cs")
    xts = [Sem(nc, f"xts{i}") for i in range(2)]
    wqks = [Sem(nc, f"wqks{i}") for i in range(2)]
    wvs = [Sem(nc, f"wvs{i}") for i in range(2)]
    oto = [Sem(nc, f"oto{i}") for i in range(2)]
    pmis = [Sem(nc, f"pmis{i}") for i in range(2)]
    pmos = [Sem(nc, f"pmos{i}") for i in range(2)]
    wos = Sem(nc, "wos")
    xns = [Sem(nc, f"xns{i}") for i in range(2)]
    otis = [Sem(nc, f"otis{i}") for i in range(2)]
    y1to = [Sem(nc, f"y1to{i}") for i in range(2)]
    y1fo = [Sem(nc, f"y1fo{i}") for i in range(2)]
    y1ti = Sem(nc, "y1ti")
    w13s = [Sem(nc, f"w13s{i}") for i in range(2)]
    hido = [Sem(nc, f"hido{i}") for i in range(2)]
    hidi = [Sem(nc, f"hidi{i}") for i in range(2)]
    w2ps = [Sem(nc, f"w2ps{i}") for i in range(2)]
    y1is = [Sem(nc, f"y1is{i}") for i in range(max(2, NSUP))]
    outs = [Sem(nc, f"outs{i}") for i in range(2)]

    def inc(sem, k=1):
        return (sem, k)

    ALLSEMS = ([tq, vq, sq, cs, wos, y1ti] + xts + wqks + wvs + oto +
               pmis + pmos + xns + otis + y1to + y1fo + w13s +
               hido + hidi + w2ps + y1is + outs)

    def phase_barrier():
        toks = [s.tok() for s in ALLSEMS]
        P.emit("v", lambda v: v.memset(nmscal[0:1, 7:8], 0.0),
               waits=toks, incs=[inc(vq)])
        btok = vq.tok()
        for k in list(state.keys()):
            v = state[k]
            if isinstance(v, list):
                state[k] = [btok for _ in v]
            else:
                state[k] = btok
        for i in range(8):
            bank_free[i] = btok
        return btok

    ZT = (tq, 0)
    bank_free = {i: ZT for i in range(8)}
    state = dict(
        xbT_free=ZT, vpair_free=ZT, qk_free=ZT,
        otstage_free=[ZT, ZT],
        wqk_read=[ZT, ZT], wv_read=[ZT, ZT],
        fold_in_free=[ZT, ZT], fold_st_free=[ZT, ZT],
        hid2_free=[ZT, ZT],
        wobuf_free=ZT, xnat_free=[ZT, ZT], ottile_free=[ZT, ZT],
        zsb_free=ZT, sqf_free=ZT, y1f_free=[ZT, ZT], y1fspill=[ZT, ZT],
        y1Tst_free=[ZT, ZT],
        lncol_free=[ZT, ZT], w13_free=[ZT, ZT], silu_free=[ZT, ZT],
        hidb_free=[ZT, ZT], hsup_free=ZT, w2blk_free=[ZT, ZT],
        y1in_free=[ZT] * NSUP, outsb_free=[ZT, ZT], z2_free=ZT,
        bcsb_free=[ZT, ZT],
        ptile_free=[ZT, ZT, ZT, ZT],
    )

    def emit_consts():
        P.emit("g", lambda g: g.dma_start(cosb[:], d_cos), incs=[inc(cs, 16)])
        P.emit("g", lambda g: g.dma_start(sinb[:], d_sin), incs=[inc(cs, 16)])
        P.emit("g", lambda g: g.dma_start(h128b[:], d_h128),
               incs=[inc(cs, 16)])
        P.emit("g", lambda g: g.dma_start(hAb[:], d_hA), incs=[inc(cs, 16)])
        P.emit("g", lambda g: g.dma_start(trib[:], d_tri), incs=[inc(cs, 16)])
        P.emit("g", lambda g: g.dma_start(identb[:], d_id),
               incs=[inc(cs, 16)])
        for i, (k, v) in enumerate(CVALS.items()):
            vv = float(v)
            P.emit("v", lambda v_, vv=vv, i=i:
                   v_.memset(consts[:, i:i + 1], vv))
        P.emit("v", lambda v: v.memset(onesf[:], 1.0))
        P.emit("v", lambda v: v.memset(onesb[:], 1.0), incs=[inc(vq)])

    n_fold = (128 // FP) * (D // FD)

    def emit_fold_chunk(ci):
        P.label = f"fold{ci}"
        par = ci % 2
        pi_ = ci // (D // FD)
        di = ci % (D // FD)
        src = d_w2.rearrange("(a p) d -> a p d", a=A)[
            :, pi_ * FP:(pi_ + 1) * FP, di * FD:(di + 1) * FD]
        P.emit("g", lambda g, src=src, par=par: g.dma_start(
            w2fin[par][:].rearrange("r (p d) -> r p d", p=FP), src),
            waits=[state["fold_in_free"][par]], incs=[inc(pmis[par], 16)])
        ldtok = (pmis[par], pmis[par].n)
        for n0 in range(0, FP * FD, 512):
            b_ = (n0 // 512) % 2
            P.emit("t", lambda t, n0=n0, b_=b_, par=par: t.matmul(
                ps[b_][0:A, 0:512], hAb[0:A, 0:A],
                w2fin[par][:, n0:n0 + 512], start=True, stop=True),
                waits=([ldtok, (cs, cs.n), bank_free[b_]] if n0 < 1024
                       else [bank_free[b_]]),
                incs=[inc(tq)])
            mt = tq.tok()
            P.emit("s", lambda s, n0=n0, b_=b_, par=par: s.copy(
                w2fst[par][:, n0:n0 + 512], ps[b_][0:A, 0:512]),
                waits=([mt, state["fold_st_free"][par]] if n0 == 0
                       else [mt]),
                incs=[inc(sq)])
            bank_free[b_] = sq.tok()
        state["fold_in_free"][par] = tq.tok()
        evtok = sq.tok()
        nwq = FH // (128 * 16)
        hpw = A // nwq
        for w in range(nwq):
            dst = d_w2pp[di, w].rearrange("p (h d) -> h p d", d=FD)[
                :, pi_ * FP:(pi_ + 1) * FP, :]
            src = w2fst[par][w * hpw:(w + 1) * hpw, :].rearrange(
                "r (p d) -> r p d", p=FP)
            P.emit("y", lambda y, dst=dst, src=src: y.dma_start(dst, src),
                   waits=([evtok] if w == 0 else ()),
                   incs=[inc(pmos[par], 16)])
        state["fold_st_free"][par] = (pmos[par], pmos[par].n)

    def emit_xt_load(b):
        for c in range(DC):
            src = d_xT[b, c * 128:(c + 1) * 128, :]
            P.emit("g", lambda g, c=c, src=src:
                   g.dma_start(xbT[:, c * S:(c + 1) * S], src),
                   waits=[state["xbT_free"]] if c == 0 else (),
                   incs=[inc(xts[b], 16)])

    def emit_pair_weight_loads(p, b):
        par = p % 2
        for (dst, src) in ((wqblk[par], d_wq), (wkblk[par], d_wk)):
            ap = src.rearrange("(c r) e -> r c e", r=128)[
                :, :, 2 * p * 128: 2 * p * 128 + 256]
            P.emit("g", lambda g, dst=dst, ap=ap: g.dma_start(
                dst[:].rearrange("r (c e) -> r c e", c=DC), ap),
                   waits=[state["wqk_read"][par]], incs=[inc(wqks[par], 16)])
        ap = d_wv.rearrange("(c r) e -> r c e", r=128)[
            :, :, 2 * p * 128: 2 * p * 128 + 256]
        P.emit("g", lambda g, ap=ap, par=par:
               g.dma_start(wvblk[par][:].rearrange("r (c e) -> r c e", c=DC),
                           ap),
               waits=[state["wv_read"][par]], incs=[inc(wvs[par], 16)])

    def qk_prod(b, wblk, wtok, oute, outo):
        for qc in range(QC):
            n0 = qc * CW
            nn = CW
            be, bo = (qc % 2) * 2, (qc % 2) * 2 + 1
            for eo, bank in ((0, be), (1, bo)):
                for c in range(DC):
                    P.emit("t", lambda t, eo=eo, bank=bank, c=c, n0=n0,
                           nn=nn, wblk=wblk: t.matmul(
                               ps[bank][:, 0:nn],
                               wblk[:, c * 256 + eo * 128:
                                    c * 256 + eo * 128 + 128],
                               xbT[:, c * S + n0:c * S + n0 + nn],
                               start=(c == 0), stop=(c == DC - 1)),
                           waits=([wtok, (xts[b], xts[b].n),
                                   bank_free[bank]] if c == 0 else ()),
                           incs=([inc(tq)] if c == DC - 1 else ()))
            mmtok = tq.tok()
            cS = cosb[:, n0:n0 + nn]
            sS = sinb[:, n0:n0 + nn]
            P.emit("v", lambda v, be=be, nn=nn, cS=cS:
                   v.tensor_tensor(rtmp[0][:, 0:nn], ps[be][:, 0:nn], cS,
                                   ALU.mult),
                   waits=[mmtok, (cs, cs.n), vq.tok()])
            P.emit("v", lambda v, bo=bo, nn=nn, sS=sS:
                   v.tensor_tensor(rtmp[1][:, 0:nn], ps[bo][:, 0:nn], sS,
                                   ALU.mult), incs=[inc(vq)])
            w1_ = vq.tok()
            P.emit("v", lambda v, nn=nn, n0=n0, oute=oute:
                   v.tensor_tensor(oute[:, n0:n0 + nn], rtmp[0][:, 0:nn],
                                   rtmp[1][:, 0:nn], ALU.subtract),
                   waits=[w1_], incs=[inc(vq)])
            subtok = vq.tok()
            P.emit("v", lambda v, be=be, nn=nn, sS=sS:
                   v.tensor_tensor(rtmp[0][:, 0:nn], ps[be][:, 0:nn], sS,
                                   ALU.mult), waits=[subtok])
            P.emit("v", lambda v, bo=bo, nn=nn, cS=cS:
                   v.tensor_tensor(rtmp[1][:, 0:nn], ps[bo][:, 0:nn], cS,
                                   ALU.mult), incs=[inc(vq)])
            w2_ = vq.tok()
            P.emit("v", lambda v, nn=nn, n0=n0, outo=outo:
                   v.tensor_tensor(outo[:, n0:n0 + nn], rtmp[0][:, 0:nn],
                                   rtmp[1][:, 0:nn], ALU.add),
                   waits=[w2_], incs=[inc(vq)])
            done = vq.tok()
            bank_free[be] = done
            bank_free[bo] = done

    def v_prod(b, par):
        wtok = (wvs[par], wvs[par].n)
        war = state["vpair_free"]
        for tc in range(TT):
            bank = 4 + tc % 2
            for c in range(DC):
                P.emit("t", lambda t, tc=tc, c=c, bank=bank, par=par:
                       t.matmul(
                           ps[bank][:, 0:256],
                           xbT[:, c * S + tc * 128:c * S + tc * 128 + 128],
                           wvblk[par][:, c * 256:(c + 1) * 256],
                           start=(c == 0), stop=(c == DC - 1)),
                       waits=([wtok, (xts[b], xts[b].n), bank_free[bank]]
                              if c == 0 else ()),
                       incs=([inc(tq)] if c == DC - 1 else ()))
            mmtok = tq.tok()
            P.emit("s", lambda s, tc=tc, bank=bank:
                   s.copy(vpair[:, tc * 256:(tc + 1) * 256],
                          ps[bank][:, 0:256]),
                   waits=[mmtok, war] if tc == 0 else [mmtok],
                   incs=[inc(sq)])
            bank_free[bank] = sq.tok()

    def emit_attention_pair(p, b, prefetch=None):
        P.label = f"attn_b{b}p{p}"
        par = p % 2
        wtokqk = (wqks[par], wqks[par].n)
        qk_prod(b, wqblk[par], wtokqk, qe, qo)
        qk_prod(b, wkblk[par], wtokqk, ke, ko)
        state["wqk_read"][par] = tq.tok()
        ropetok = vq.tok()
        v_prod(b, par)
        state["wv_read"][par] = tq.tok()
        vtok = sq.tok()
        if prefetch is not None:
            emit_pair_weight_loads(*prefetch)
        ptoks = {}
        rctoks = {}

        def emit_scores(qc, kc):
            q0 = qc * CW
            qn = CW
            k0 = kc * 128
            for hb in (0, 1):
                pb = 64 * hb
                sbank = hb * 2 + kc % 2
                P.emit("t", lambda t, sbank=sbank, pb=pb, k0=k0, q0=q0,
                       qn=qn: t.matmul(
                           ps[sbank][:, 0:qn],
                           ke[pb:pb + 64, k0:k0 + 128],
                           qe[pb:pb + 64, q0:q0 + qn],
                           start=True, stop=False),
                       waits=([bank_free[sbank], ropetok]
                              if (qc == 0 and kc == 0)
                              else [bank_free[sbank]]))
            for hb in (0, 1):
                pb = 64 * hb
                sbank = hb * 2 + kc % 2
                P.emit("t", lambda t, sbank=sbank, pb=pb, k0=k0, q0=q0,
                       qn=qn: t.matmul(
                           ps[sbank][:, 0:qn],
                           ko[pb:pb + 64, k0:k0 + 128],
                           qo[pb:pb + 64, q0:q0 + qn],
                           start=False, stop=True),
                       incs=[inc(tq)])
                smm = tq.tok()
                pi = hb * 2 + kc % 2
                pt = ptile[pi]
                P.emit("s", lambda s, pt=pt, sbank=sbank, qn=qn:
                       s.activation(pt[:, 0:qn], ps[sbank][:, 0:qn],
                                    AF.Exp, scale=ISQ128),
                       waits=[smm, state["ptile_free"][pi]],
                       incs=[inc(sq)])
                exptok = sq.tok()
                bank_free[sbank] = exptok
                ptok = exptok
                qoff = k0 - q0
                if qoff >= 0:
                    if qoff > 0:
                        P.emit("v", lambda v, pt=pt, qoff=qoff:
                               v.memset(pt[:, 0:qoff], 0.0),
                               waits=[exptok])
                    P.emit("v", lambda v, pt=pt, qoff=qoff:
                           v.tensor_tensor(
                               pt[:, qoff:qoff + 128],
                               pt[:, qoff:qoff + 128], trib[:],
                               ALU.mult),
                           waits=([exptok] if qoff == 0 else ()),
                           incs=[inc(vq)])
                    ptok = vq.tok()
                ptoks[(kc, hb)] = ptok

        def emit_sumpv(qc, kc):
            qn = CW
            kcmax = (qc + 1) * KPQ
            last = kc == kcmax - 1
            if last:
                for hb in (0, 1):
                    pi = hb * 2 + kc % 2
                    pt = ptile[pi]
                    rb = 6 + hb
                    P.emit("t", lambda t, rb=rb, pt=pt, qn=qn: t.matmul(
                        ps[rb][0:1, 0:qn], onesb[:, 0:1],
                        pt[:, 0:qn], start=False, stop=True),
                        waits=[ptoks[(kc, hb)]], incs=[inc(tq)])
                    rstok = tq.tok()
                    P.emit("s", lambda s, rb=rb, qn=qn, hb=hb: s.activation(
                        bcsb[hb][0:1, 0:qn], ps[rb][0:1, 0:qn], AF.Ln),
                        waits=[rstok, state["bcsb_free"][hb]],
                        incs=[inc(sq)])
                    P.emit("s", lambda s, qn=qn, hb=hb: s.activation(
                        bcsb[hb][0:1, 0:qn], bcsb[hb][0:1, 0:qn], AF.Exp,
                        scale=-1.0),
                        incs=[inc(sq)])
                    rctoks[hb] = sq.tok()
                for hb in (0, 1):
                    pi = hb * 2 + kc % 2
                    pt = ptile[pi]
                    ob = 4 + hb
                    P.emit("t", lambda t, ob=ob, pt=pt, qn=qn, kc=kc,
                           hb=hb: t.matmul(
                               ps[ob][:, 0:qn],
                               vpair[:, kc * 256 + hb * 128:
                                     kc * 256 + hb * 128 + 128],
                               pt[:, 0:qn], start=False, stop=True),
                           incs=[inc(tq)])
                    state["ptile_free"][pi] = tq.tok()
                return
            for hb in (0, 1):
                pi = hb * 2 + kc % 2
                pt = ptile[pi]
                rb = 6 + hb
                ob = 4 + hb
                P.emit("t", lambda t, rb=rb, pt=pt, qn=qn, kc=kc:
                       t.matmul(
                           ps[rb][0:1, 0:qn], onesb[:, 0:1],
                           pt[:, 0:qn], start=(kc == 0), stop=False),
                       waits=([ptoks[(kc, hb)], bank_free[rb]] if kc == 0
                              else [ptoks[(kc, hb)]]))
                P.emit("t", lambda t, ob=ob, pt=pt, qn=qn, kc=kc, hb=hb:
                       t.matmul(
                           ps[ob][:, 0:qn],
                           vpair[:, kc * 256 + hb * 128:
                                 kc * 256 + hb * 128 + 128],
                           pt[:, 0:qn], start=(kc == 0), stop=False),
                       waits=([vtok, bank_free[ob]] if kc == 0 else ()),
                       incs=[inc(tq)])
                state["ptile_free"][pi] = tq.tok()

        def emit_cluster(qc):
            q0 = qc * CW
            qn = CW
            for hb in (0, 1):
                h = 2 * p + hb
                rb = 6 + hb
                ob = 4 + hb
                ost = otstage[hb]
                P.emit("t", lambda t, rb=rb, qn=qn, hb=hb: t.matmul(
                    ps[rb][:, 0:qn], onesf[:], bcsb[hb][0:1, 0:qn],
                    start=True, stop=True),
                    waits=[rctoks[hb]], incs=[inc(tq)])
                bctok = tq.tok()
                P.emit("s", lambda s, rb=rb, qn=qn, hb=hb: s.copy(
                    bcsb[hb][:, 0:qn], ps[rb][:, 0:qn]),
                    waits=[bctok], incs=[inc(sq)])
                bank_free[rb] = sq.tok()
                bcev = sq.tok()
                P.emit("v", lambda v, ob=ob, q0=q0, qn=qn, ost=ost,
                       hb=hb: v.tensor_tensor(
                           ost[:, q0:q0 + qn], ps[ob][:, 0:qn],
                           bcsb[hb][:, 0:qn], ALU.mult),
                       waits=[bcev] + ([state["otstage_free"][hb]]
                                       if qc == 0 else []),
                       incs=[inc(vq)])
                bank_free[ob] = vq.tok()
                state["bcsb_free"][hb] = vq.tok()
                if qc == QC - 1:
                    oev = vq.tok()
                    P.emit("y", lambda y, h=h, ost=ost, b=b: y.dma_start(
                        d_oT[b, h], ost[:]), waits=[oev],
                        incs=[inc(oto[hb], 16)])
                    state["otstage_free"][hb] = (oto[hb], oto[hb].n)

        tiles = [(qc, kc) for qc in range(QC)
                 for kc in range((qc + 1) * KPQ)]
        prev = None
        pend_cl = None
        for (qc, kc) in tiles:
            emit_scores(qc, kc)
            if prev is not None:
                if pend_cl is not None:
                    emit_cluster(pend_cl)
                    pend_cl = None
                emit_sumpv(*prev)
                if prev[0] != qc:
                    pend_cl = prev[0]
            prev = (qc, kc)
        emit_sumpv(*prev)
        emit_cluster(QC - 1)
        state["qk_free"] = tq.tok()
        state["vpair_free"] = tq.tok()

    def emit_wo_load():
        ap = d_wo.rearrange("(c r) e -> r c e", r=128)
        P.emit("g", lambda g, ap=ap: g.dma_start(
            wobuf[:].rearrange("r (c e) -> r c e", c=DC), ap),
            waits=[state["wobuf_free"]], incs=[inc(wos, 16)])

    def emit_ln(zview, sqb, outf_list, ztok, par, extra_war):
        col = lncol[:, par * 4:par * 4 + 4]
        P.emit("v", lambda v: v.reduce_sum(col[:, 0:1], zview, axis=AX.X),
               waits=[ztok, state["lncol_free"][par]], incs=[inc(vq)])
        t1 = vq.tok()
        P.emit("v", lambda v: v.tensor_scalar(
            col[:, 1:2], col[:, 0:1], 1.0 / D, None, ALU.mult),
            waits=[t1], incs=[inc(vq)])
        t2 = vq.tok()
        P.emit("v", lambda v: v.tensor_scalar(
            zview, zview, col[:, 1:2], None, ALU.subtract),
            waits=[t2], incs=[inc(vq)])
        t3 = vq.tok()
        P.emit("s", lambda s: s.activation(sqb[:], zview, AF.Square),
               waits=[t3, state["sqf_free"]], incs=[inc(sq)])
        t4 = sq.tok()
        P.emit("v", lambda v: v.reduce_sum(col[:, 2:3], sqb[:], axis=AX.X),
               waits=[t4], incs=[inc(vq)])
        t5 = vq.tok()
        state["sqf_free"] = t5
        P.emit("s", lambda s: s.activation(
            col[:, 3:4], col[:, 2:3], AF.Ln, scale=CVALS["invD"],
            bias=CVALS["eps"]), waits=[t5], incs=[inc(sq)])
        t6 = sq.tok()
        P.emit("s", lambda s: s.activation(
            col[:, 3:4], col[:, 3:4], AF.Exp, scale=CVALS["nhalf"]),
            waits=[t6], incs=[inc(sq)])
        t7 = sq.tok()
        for outf, war in zip(outf_list, extra_war):
            P.emit("v", lambda v, outf=outf: v.tensor_scalar(
                outf[:], zview, col[:, 3:4], None, ALU.mult),
                waits=[t7] + list(war), incs=[inc(vq)])
        state["lncol_free"][par] = vq.tok()

    pend_tr = []

    def emit_transposes(b, tt, par, lntok, sptok):
        for c in range(DC):
            tb = 4 + c % 4
            P.emit("t", lambda t, c=c, par=par, tb=tb: t.transpose(
                ps[tb][:, 0:128], y1f[par][:, c * 128:(c + 1) * 128],
                identb[:]),
                waits=([lntok, bank_free[tb], (cs, cs.n)] if c < 4
                       else [bank_free[tb]]),
                incs=[inc(tq)])
            trtok = tq.tok()
            if c % 2 == 0:
                P.emit("s", lambda s, c=c, par=par, tb=tb: s.copy(
                    y1Tst[par][:, c * 128:(c + 1) * 128], ps[tb][:, 0:128]),
                    waits=([trtok, state["y1Tst_free"][par]] if c < 2
                           else [trtok]),
                    incs=[inc(sq)])
                bank_free[tb] = sq.tok()
            else:
                P.emit("v", lambda v, c=c, par=par, tb=tb: v.tensor_copy(
                    y1Tst[par][:, c * 128:(c + 1) * 128], ps[tb][:, 0:128]),
                    waits=([trtok, state["y1Tst_free"][par]] if c < 2
                           else [trtok]),
                    incs=[inc(vq)])
                bank_free[tb] = vq.tok()
        trev = [sq.tok(), vq.tok()]
        state["y1f_free"][par] = trev
        dst = d_y1T[:, :, b * S + tt * 128: b * S + (tt + 1) * 128]
        P.emit("g", lambda g, dst=dst, par=par: g.dma_start(
            dst.rearrange("c r e -> r c e"),
            y1Tst[par][:].rearrange("r (c e) -> r c e", c=DC)),
            waits=[trev, sptok], incs=[inc(y1to[par], 16)])
        state["y1Tst_free"][par] = (y1to[par], y1to[par].n)

    def emit_w_in_dma(b, tt):
        par = tt % 2
        P.emit("y", lambda y, b=b, tt=tt, par=par: y.dma_start(
            xnat[par][:], d_xn[b, tt * 128:(tt + 1) * 128, :]),
            waits=[state["xnat_free"][par]], incs=[inc(xns[par], 16)])
        otap = d_oT[b, :, :, tt * 128:(tt + 1) * 128]
        P.emit("y", lambda y, otap=otap, par=par: y.dma_start(
            ottile[par][:].rearrange("r (h e) -> r h e", h=NH),
            otap.rearrange("h r e -> r h e")),
            waits=[state["ottile_free"][par],
                   (oto[0], oto[0].n), (oto[1], oto[1].n)],
            incs=[inc(otis[par], 16)])
        return ((xns[par], xns[par].n), (otis[par], otis[par].n))

    def emit_w_phase(b):
        P.label = f"W{b}"
        emit_wo_load()
        wotok = (wos, wos.n)
        ndc = D // 512
        toks = emit_w_in_dma(b, 0)
        for tt in range(TT):
            par = tt % 2
            P.label = f"W{b}t{tt}"
            xtok, ottok = toks
            for h in range(NH):
                for dc in range(ndc):
                    P.emit("t", lambda t, h=h, dc=dc, par=par: t.matmul(
                        ps[dc][:, :],
                        ottile[par][:, h * 128:(h + 1) * 128],
                        wobuf[:, h * D + dc * 512: h * D + dc * 512 + 512],
                        start=(h == 0), stop=(h == NH - 1)),
                        waits=([wotok, ottok] +
                               [bank_free[i] for i in range(ndc)])
                        if (h == 0 and dc == 0) else (),
                        incs=([inc(tq)]
                              if (h == NH - 1 and dc == ndc - 1) else ()))
            wtok = tq.tok()
            state["ottile_free"][par] = wtok
            if tt + 1 < TT:
                toks = emit_w_in_dma(b, tt + 1)
            for dc in range(ndc):
                P.emit("v", lambda v, dc=dc, par=par: v.tensor_tensor(
                    zsb[:, dc * 512:(dc + 1) * 512], ps[dc][:, :],
                    xnat[par][:, dc * 512:(dc + 1) * 512], ALU.add),
                    waits=([wtok, xtok, state["zsb_free"]] if dc == 0
                           else []),
                    incs=([inc(vq)] if dc == ndc - 1 else ()))
            ztok = vq.tok()
            for dc in range(ndc):
                bank_free[dc] = ztok
            state["xnat_free"][par] = ztok
            emit_ln(zsb[:], sqf, [y1f[par]], ztok, par,
                    [[state["y1f_free"][par], state["y1fspill"][par]]])
            lntok = vq.tok()
            state["zsb_free"] = lntok
            P.emit("g", lambda g, b=b, tt=tt, par=par: g.dma_start(
                d_y1f[b * S + tt * 128: b * S + (tt + 1) * 128, :],
                y1f[par][:]),
                waits=[lntok], incs=[inc(y1fo[par], 16)])
            sptok = (y1fo[par], y1fo[par].n)
            state["y1fspill"][par] = sptok
            pend_tr.append((b, tt, par, lntok, sptok))
            if len(pend_tr) > 1:
                emit_transposes(*pend_tr.pop(0))
        while pend_tr:
            emit_transposes(*pend_tr.pop(0))
        state["wobuf_free"] = tq.tok()

    def emit_u_phase():
        P.label = "U"
        uw = [(y1to[0], y1to[0].n), (y1to[1], y1to[1].n),
              state["xbT_free"], state["wobuf_free"]]
        for c in range(DC):
            P.emit("y" if c % 2 == 0 else "g",
                   lambda e, c=c: e.dma_start(
                y1T[:, c * B * S:(c + 1) * B * S], d_y1T[c]),
                waits=(uw if c < 2 else ()), incs=[inc(y1ti, 16)])
        y1ttok = (y1ti, y1ti.n)
        ntc = B * S // 512
        pend_h = []

        def emit_h128(hc, tc, multok):
            tb = 4 + tc % 2
            hb_ = hidb[tc % 2]
            P.emit("t", lambda t, hb_=hb_, tb=tb: t.matmul(
                ps[tb][:, :], h128b[:], hb_[:], start=True, stop=True),
                waits=[multok, bank_free[tb]], incs=[inc(tq)])
            htok = tq.tok()
            state["hidb_free"][tc % 2] = htok
            h2 = hid2b[tc % 2]
            P.emit("s", lambda s, h2=h2, tb=tb: s.copy(
                h2[:], ps[tb][:, :]),
                waits=[htok, state["hid2_free"][tc % 2]],
                incs=[inc(sq)])
            bank_free[tb] = sq.tok()
            h2ev = sq.tok()
            dsth = d_hid[:, hc, tc * 512:(tc + 1) * 512]
            P.emit("g", lambda g, dsth=dsth, h2=h2: g.dma_start(
                dsth, h2[:]),
                waits=[h2ev], incs=[inc(hido[tc % 2], 16)])
            state["hid2_free"][tc % 2] = (hido[tc % 2],
                                          hido[tc % 2].n)

        for hc in range(A):
            par = hc % 2
            for (dst, src, war) in ((w1blk[par], d_w1,
                                     state["w13_free"][par]),
                                    (w3blk[par], d_w3, ZT)):
                ap = src.rearrange("(c r) e -> r c e", r=128)[
                    :, :, hc * 128:(hc + 1) * 128]
                P.emit("g", lambda g, dst=dst, ap=ap: g.dma_start(
                    dst[:].rearrange("r (c e) -> r c e", c=DC), ap),
                       waits=[war], incs=[inc(w13s[par], 16)])
            wtok = (w13s[par], w13s[par].n)
            for tc in range(ntc):
                gb = tc % 2
                lb = 2 + tc % 2
                for c in range(DC):
                    P.emit("t", lambda t, c=c, gb=gb, par=par, tc=tc:
                           t.matmul(ps[gb][:, :],
                                    w1blk[par][:, c * 128:(c + 1) * 128],
                                    y1T[:, c * B * S + tc * 512:
                                        c * B * S + tc * 512 + 512],
                                    start=(c == 0), stop=(c == DC - 1)),
                           waits=([wtok, y1ttok, bank_free[gb]]
                                  if c == 0 else ()))
                if pend_h:
                    emit_h128(*pend_h.pop(0))
                for c in range(DC):
                    P.emit("t", lambda t, c=c, lb=lb, par=par, tc=tc:
                           t.matmul(ps[lb][:, :],
                                    w3blk[par][:, c * 128:(c + 1) * 128],
                                    y1T[:, c * B * S + tc * 512:
                                        c * B * S + tc * 512 + 512],
                                    start=(c == 0), stop=(c == DC - 1)),
                           waits=([bank_free[lb]] if c == 0 else ()),
                           incs=([inc(tq)] if c == DC - 1 else ()))
                mmtok = tq.tok()
                sb_ = silub[tc % 2]
                hb_ = hidb[tc % 2]
                P.emit("s", lambda s, gb=gb, sb_=sb_: s.activation(
                    sb_[:], ps[gb][:, :], AF.Silu),
                    waits=[mmtok, state["silu_free"][tc % 2]],
                    incs=[inc(sq)])
                silutok = sq.tok()
                bank_free[gb] = silutok
                P.emit("v", lambda v, lb=lb, sb_=sb_, hb_=hb_:
                       v.tensor_tensor(hb_[:], sb_[:], ps[lb][:, :],
                                       ALU.mult),
                       waits=[silutok, state["hidb_free"][tc % 2]],
                       incs=[inc(vq)])
                multok = vq.tok()
                bank_free[lb] = multok
                state["silu_free"][tc % 2] = multok
                pend_h.append((hc, tc, multok))
            state["w13_free"][par] = tq.tok()
        while pend_h:
            emit_h128(*pend_h.pop(0))

    def emit_d_phase():
        P.label = "D"
        NSUPS = B * TT // NSUP
        hid_all = [(hido[0], hido[0].n), (hido[1], hido[1].n)]
        pm_all = [(pmos[0], pmos[0].n), (pmos[1], pmos[1].n)]
        y1f_all = [(y1fo[0], y1fo[0].n), (y1fo[1], y1fo[1].n)]
        ndc4 = D // 512
        NWQ = A // WQ
        for sup in range(NSUPS):
            spar = sup % 2
            for half in range(2):
                a0 = half * (A // 2)
                src = d_hid[:, a0:a0 + A // 2,
                            sup * NSUP * 128:(sup + 1) * NSUP * 128]
                P.emit("g" if half == 0 else "y",
                       lambda e, src=src, a0=a0: e.dma_start(
                           hsup[:, a0 * 512:(a0 + A // 2) * 512].rearrange(
                               "r (a e) -> r a e", a=A // 2), src),
                       waits=(hid_all + [state["hsup_free"]]),
                       incs=[inc(hidi[spar], 16)])
            for j in range(NSUP):
                tt = sup * NSUP + j
                P.emit("y", lambda y, tt=tt, j=j: y.dma_start(
                    y1in[j][:], d_y1f[tt * 128:(tt + 1) * 128, :]),
                    waits=[state["y1in_free"][j]] + (y1f_all
                                                     if sup == 0 else []),
                    incs=[inc(y1is[j], 16)])
            htok = (hidi[spar], hidi[spar].n)
            ytoks = [(y1is[j], y1is[j].n) for j in range(NSUP)]
            for dc4 in range(ndc4):
                for wqi in range(NWQ):
                    wpar = wqi % 2
                    ap = d_w2pp[dc4, wqi]
                    P.emit("y" if wpar == 0 else "g",
                           lambda e, ap=ap, wpar=wpar: e.dma_start(
                        w2blk[wpar][:], ap),
                        waits=[state["w2blk_free"][wpar]] +
                        (pm_all if sup == 0 and dc4 == 0 else []),
                        incs=[inc(w2ps[wpar], 16)])
                    wtok = (w2ps[wpar], w2ps[wpar].n)
                    for hoff in range(WQ):
                        hc = wqi * WQ + hoff
                        for j in range(NSUP):
                            P.emit("t", lambda t, hc=hc, j=j, hoff=hoff,
                                   wpar=wpar: t.matmul(
                                       ps[j][:, :],
                                       hsup[:, hc * 512 + j * 128:
                                            hc * 512 + j * 128 + 128],
                                       w2blk[wpar][:, hoff * 512:
                                                   hoff * 512 + 512],
                                       start=(hc == 0), stop=(hc == A - 1)),
                                   waits=([htok, wtok] +
                                          [bank_free[i]
                                           for i in range(NSUP)])
                                   if hc == 0 and j == 0 else
                                   ([wtok] if hoff == 0 and j == 0
                                    else ()),
                                   incs=([inc(tq)]
                                         if (hoff == WQ - 1
                                             and j == NSUP - 1)
                                         else ()))
                    state["w2blk_free"][wpar] = tq.tok()
                mmtok = tq.tok()
                for j in range(NSUP):
                    tt = sup * NSUP + j
                    P.emit("v", lambda v, j=j, dc4=dc4:
                           v.tensor_tensor(
                               z2sb[:, j * D + dc4 * 512:
                                    j * D + dc4 * 512 + 512],
                               ps[j][:, :],
                               y1in[j][:, dc4 * 512:(dc4 + 1) * 512],
                               ALU.add),
                           waits=[mmtok, ytoks[j]] +
                           ([state["z2_free"]] if j == 0 and dc4 == 0
                            else []),
                           incs=[inc(vq)])
                    bank_free[j] = vq.tok()
            z2tok = vq.tok()
            for j in range(NSUP):
                state["y1in_free"][j] = z2tok
                tt = sup * NSUP + j
                b = tt // TT
                trow = tt % TT
                par = tt % 2
                zview = z2sb[:, j * D:(j + 1) * D]
                emit_ln(zview, sqfD, [outsb[par]], z2tok, par,
                        [[state["outsb_free"][par]]])
                otok = vq.tok()
                P.emit("y", lambda y, b=b, trow=trow, par=par: y.dma_start(
                    d_y[b, trow * 128:(trow + 1) * 128, :], outsb[par][:]),
                    waits=[otok], incs=[inc(outs[par], 16)])
                state["outsb_free"][par] = (outs[par], outs[par].n)
            state["z2_free"] = vq.tok()
            state["hsup_free"] = tq.tok()

    emit_consts()
    pm_per_pair = max(1, (n_fold + B * PAIRS - 1) // (B * PAIRS))
    pm_next = [0]
    for b in range(B):
        emit_pair_weight_loads(0, b)
        emit_xt_load(b)
        for p in range(PAIRS):
            for _ in range(pm_per_pair):
                if pm_next[0] < n_fold:
                    emit_fold_chunk(pm_next[0])
                    pm_next[0] += 1
            nxt = (p + 1, b) if p + 1 < PAIRS else None
            emit_attention_pair(p, b, nxt)
        while b == B - 1 and pm_next[0] < n_fold:
            emit_fold_chunk(pm_next[0])
            pm_next[0] += 1
        phase_barrier()
        emit_w_phase(b)
        state["xbT_free"] = tq.tok()
        if b < B - 1:
            phase_barrier()
    phase_barrier()
    emit_u_phase()
    phase_barrier()
    emit_d_phase()
    fin = [(outs[0], outs[0].n), (outs[1], outs[1].n)]
    P.emit("y", lambda y: None, waits=fin)
    P.run()
    return nc


def _hadamard(n):
    H = np.array([[1.0]], dtype=np.float32)
    while H.shape[0] < n:
        H = np.block([[H, H], [H, -H]])
    return H


def host_consts(cfg: Cfg):
    S, A = cfg.S, cfg.A
    inv_freq = 1.0 / (10000.0 ** (np.arange(0, 128, 2, dtype=np.float32)
                                  / 128.0))
    t = np.arange(S, dtype=np.float32)
    fr = np.outer(inv_freq, t)
    cos = np.cos(fr).astype(np.float32)
    sin = np.sin(fr).astype(np.float32)
    cosd = np.concatenate([cos, cos], axis=0)
    sind = np.concatenate([sin, sin], axis=0)
    h128c = (_hadamard(128) * (2.0 ** -3.5)).astype(np.float32)
    AA = min(A, 64)
    hA = (_hadamard(AA) * (2.0 ** -3)).astype(np.float32)
    hAc = np.zeros((128, AA), dtype=np.float32)
    hAc[0:AA] = hA
    if A >= 64:
        hAc[64:128] = hA
    identc = np.eye(128, dtype=np.float32)
    tric = np.triu(np.ones((128, 128), dtype=np.float32))
    return dict(cosd=cosd, sind=sind, h128c=h128c, hAc=hAc, identc=identc,
                tric=tric)


def qk_perm(cfg: Cfg):
    perm = np.zeros(cfg.D, dtype=np.int64)
    for p in range(cfg.PAIRS):
        for eo in range(2):
            ci = 2 * p + eo
            for hb in range(2):
                h = 2 * p + hb
                for i in range(64):
                    perm[ci * 128 + hb * 64 + i] = h * 128 + 2 * i + eo
    return perm


def kernel(L0_attn_x, L0_attn_Wq, L0_attn_Wk, L0_attn_Wv, L0_attn_Wo,
           L0_ffn_W1, L0_ffn_W2, L0_ffn_W3):
    cfg = Cfg()
    x = np.ascontiguousarray(np.asarray(L0_attn_x, dtype=np.float32))
    Wq = np.asarray(L0_attn_Wq, np.float32)
    Wk = np.asarray(L0_attn_Wk, np.float32)
    Wv = np.asarray(L0_attn_Wv, np.float32)
    Wo = np.asarray(L0_attn_Wo, np.float32)
    W1 = np.asarray(L0_ffn_W1, np.float32)
    W2 = np.asarray(L0_ffn_W2, np.float32)
    W3 = np.asarray(L0_ffn_W3, np.float32)

    qperm = qk_perm(cfg)
    shared = host_consts(cfg)
    shared["wqT_p"] = np.ascontiguousarray(Wq.T[:, qperm])
    shared["wkT_p"] = np.ascontiguousarray(Wk.T[:, qperm])
    shared["wvT"] = np.ascontiguousarray(Wv.T)
    shared["woT"] = np.ascontiguousarray(Wo.T)
    shared["w1T_p"] = np.ascontiguousarray(W1.T)
    shared["w3T_p"] = np.ascontiguousarray(W3.T)
    shared["w2T"] = np.ascontiguousarray(W2.T)

    nc = build(cfg)
    in_maps = []
    for c in range(cfg.NCORES):
        d = dict(shared)
        xs = x[c * cfg.B:(c + 1) * cfg.B]
        d["xT"] = np.ascontiguousarray(xs.transpose(0, 2, 1))
        d["x_nat"] = np.ascontiguousarray(xs)
        in_maps.append(d)
    res = run_bass_kernel_spmd(nc, in_maps, core_ids=list(range(cfg.NCORES)))
    out = np.concatenate([np.asarray(r["y"], np.float32)
                          for r in res.results], axis=0)
    return out


# version 13
# speedup vs baseline: 1.0743x; 1.0743x over previous
import math
import numpy as np
import concourse.bass as bass
import concourse.mybir as mybir
from concourse.bass_utils import run_bass_kernel_spmd

F32 = mybir.dt.float32
BF16 = mybir.dt.bfloat16
AF = mybir.ActivationFunctionType
ALU = mybir.AluOpType
AX = mybir.AxisListType

ISQ128 = float(1.0 / math.sqrt(128.0))


class Cfg:
    def __init__(self, S=2048, NH=16, FH=8192, B=2, NCORES=8):
        self.S = S
        self.NH = NH
        self.D = NH * 128
        self.FH = FH
        self.B = B
        self.NCORES = NCORES
        self.PAIRS = NH // 2
        self.DC = self.D // 128
        self.A = FH // 128
        self.QC = max(1, S // 512)
        self.TT = S // 128
        self.KPQ = (S // self.QC) // 128
        self.NSUP = 4 if (B * self.TT) % 4 == 0 else 1


class Sem:
    __slots__ = ("h", "n", "name")

    def __init__(self, nc, name):
        self.h = nc.alloc_semaphore(name)
        self.n = 0
        self.name = name

    def tok(self):
        return (self, self.n)


class Prog:
    ENGS = ("t", "v", "s", "g", "y")

    def __init__(self, nc):
        self.nc = nc
        self.q = {e: [] for e in self.ENGS}
        self.label = ""
        self.hwm = {e: {} for e in self.ENGS}

    def emit(self, e, fn, waits=(), incs=()):
        ws = []
        hw = self.hwm[e]

        def _flat(items):
            for w in items:
                if w is None:
                    continue
                if isinstance(w, list):
                    _flat(w)
                elif w[1] > hw.get(w[0], 0):
                    hw[w[0]] = w[1]
                    ws.append(w)
        _flat(waits)
        for s, k in incs:
            s.n += k
        self.q[e].append((fn, tuple(ws), tuple(incs), self.label))

    def run(self):
        block_map = {"t": "tensor", "v": "vector", "s": "scalar",
                     "g": "gpsimd", "y": "sync"}
        with self.nc.Block() as block:
            for e in self.ENGS:
                ops = self.q[e]

                def make_runner(ops):
                    def _(eng):
                        for fn, ws, incs, lbl in ops:
                            for s, v in ws:
                                eng.wait_ge(s.h, v)
                            inst = fn(eng)
                            if inst is not None and lbl:
                                inst.annotate(lbl)
                            for s, k in incs:
                                inst = inst.then_inc(s.h, k)
                    return _
                getattr(block, block_map[e])(make_runner(ops))


def build(cfg: Cfg):
    nc = bass.Bass("TRN2", target_bir_lowering=False)
    S, D, NH, FH, B = cfg.S, cfg.D, cfg.NH, cfg.FH, cfg.B
    DC, A, QC, TT, KPQ = cfg.DC, cfg.A, cfg.QC, cfg.TT, cfg.KPQ
    PAIRS, NSUP = cfg.PAIRS, cfg.NSUP
    NCH = S // 512 if S >= 512 else 1
    CW = min(512, S)

    d_xT = nc.dram_tensor("xT", [B, D, S], F32, kind="ExternalInput").ap()
    d_xn = nc.dram_tensor("x_nat", [B, S, D], F32, kind="ExternalInput").ap()
    d_wq = nc.dram_tensor("wqT_p", [D, D], F32, kind="ExternalInput").ap()
    d_wk = nc.dram_tensor("wkT_p", [D, D], F32, kind="ExternalInput").ap()
    d_wv = nc.dram_tensor("wvT", [D, D], F32, kind="ExternalInput").ap()
    d_wo = nc.dram_tensor("woT", [D, D], F32, kind="ExternalInput").ap()
    d_w1 = nc.dram_tensor("w1T_p", [D, FH], F32, kind="ExternalInput").ap()
    d_w3 = nc.dram_tensor("w3T_p", [D, FH], F32, kind="ExternalInput").ap()
    d_w2 = nc.dram_tensor("w2T", [FH, D], F32, kind="ExternalInput").ap()
    d_cos = nc.dram_tensor("cosd", [128, S], F32, kind="ExternalInput").ap()
    d_sin = nc.dram_tensor("sind", [128, S], F32, kind="ExternalInput").ap()
    d_h128 = nc.dram_tensor("h128c", [128, 128], F32, kind="ExternalInput").ap()
    d_hA = nc.dram_tensor("hAc", [128, min(A, 64)], F32,
                          kind="ExternalInput").ap()
    d_id = nc.dram_tensor("identc", [128, 128], F32, kind="ExternalInput").ap()
    d_tri = nc.dram_tensor("tric", [128, 128], F32, kind="ExternalInput").ap()
    d_y = nc.dram_tensor("y", [B, S, D], F32, kind="ExternalOutput").ap()
    d_w2pp = nc.dram_tensor("w2pp_s", [D // 512, FH // (128 * 16), 128,
                                       16 * 512], BF16, kind="Internal").ap()
    d_hid = nc.dram_tensor("hid_s", [128, A, B * S], BF16,
                           kind="Internal").ap()
    d_y1T = nc.dram_tensor("y1T_s", [DC, 128, B * S], BF16,
                           kind="Internal").ap()
    d_y1f = nc.dram_tensor("y1f_s", [B * S, D], F32, kind="Internal").ap()
    d_oT = nc.dram_tensor("oT_s", [B, NH, 128, S], BF16, kind="Internal").ap()

    SBF = {}
    OFFS = {}
    _off = [16640]

    def At(name, shape, dtype, off=None):
        if off is None:
            off = _off[0]
            per = int(np.prod(shape[1:])) * (2 if dtype == BF16 else 4)
            _off[0] = off + ((per + 31) // 32) * 32
        t = nc.alloc_sbuf_tensor_at(name, shape, dtype, offset=off)
        SBF[name] = t[:]
        OFFS[name] = off
        return t[:]

    def size_of(shape, dtype):
        return int(np.prod(shape[1:])) * (2 if dtype == BF16 else 4)

    consts = At("consts", [128, 16], F32)
    onesf = At("onesf", [1, 128], F32)
    onesb = At("onesb", [128, 128], BF16)
    h128b = At("h128b", [128, 128], BF16)
    hAb = At("hAb", [128, min(A, 64)], BF16)
    identb = At("identf", [128, 128], F32)
    trib = At("trib", [128, 128], BF16)
    lncol = At("lncol", [128, 8], F32)
    nmscal = At("nmscal", [1, 8], F32)
    cosb = At("cosb", [128, S], BF16)
    sinb = At("sinb", [128, S], BF16)
    PBASE = ((_off[0] + 63) // 64) * 64

    P1 = PBASE
    P1_SZ = size_of([128, DC * S], BF16)
    P2 = P1 + P1_SZ
    o = P2
    def nxt(shape, dtype):
        nonlocal o
        off = o
        o += ((size_of(shape, dtype) + 31) // 32) * 32
        return off
    xbT = At("xbT", [128, DC * S], BF16, P1)
    qe = At("qe", [128, S], BF16, nxt([128, S], BF16))
    qo = At("qo", [128, S], BF16, nxt([128, S], BF16))
    ke = At("ke", [128, S], BF16, nxt([128, S], BF16))
    ko = At("ko", [128, S], BF16, nxt([128, S], BF16))
    vpair = At("vpair", [128, TT * 256], BF16, nxt([128, TT * 256], BF16))
    wqblk = [At(f"wqblk{i}", [128, DC * 256], BF16,
                nxt([128, DC * 256], BF16)) for i in range(2)]
    wkblk = [At(f"wkblk{i}", [128, DC * 256], BF16,
                nxt([128, DC * 256], BF16)) for i in range(2)]
    wvblk = [At(f"wvblk{i}", [128, DC * 256], BF16,
                nxt([128, DC * 256], BF16)) for i in range(2)]
    ptile = [At(f"ptile{i}", [128, CW], BF16, nxt([128, CW], BF16))
             for i in range(4)]
    rtmp = [At(f"rtmp{i}", [128, CW], F32, nxt([128, CW], F32))
            for i in range(2)]
    bcsb = [At(f"bcsb{i}", [128, CW], F32, nxt([128, CW], F32))
            for i in range(2)]
    otstage = [At(f"otstage{i}", [128, S], BF16, nxt([128, S], BF16))
               for i in range(2)]
    FP = 8
    FD = 512
    w2fin = [At(f"w2fin{i}", [A, FP * FD], BF16, nxt([A, FP * FD], BF16))
             for i in range(2)]
    w2fst = [At(f"w2fst{i}", [A, FP * FD], BF16, nxt([A, FP * FD], BF16))
             for i in range(2)]
    A_END = o
    o = P2
    wobuf = At("wobuf", [128, DC * D], BF16, nxt([128, DC * D], BF16))
    xnat = [At(f"xnat{i}", [128, D], F32, nxt([128, D], F32))
            for i in range(2)]
    ottile = [At(f"ottile{i}", [128, NH * 128], BF16,
                 nxt([128, NH * 128], BF16)) for i in range(2)]
    zsb = At("zsb", [128, D], F32, nxt([128, D], F32))
    sqf = At("sqf", [128, D], F32, nxt([128, D], F32))
    y1f = [At(f"y1f{i}", [128, D], F32, nxt([128, D], F32))
           for i in range(2)]
    y1Tst = [At(f"y1Tst{i}", [128, DC * 128], BF16,
                nxt([128, DC * 128], BF16)) for i in range(2)]
    W_END = o
    y1T = At("y1T", [128, DC * B * S], BF16, P1)
    o = P1 + size_of([128, DC * B * S], BF16)
    w1blk = [At(f"w1blk{i}", [128, DC * 128], BF16,
                nxt([128, DC * 128], BF16)) for i in range(2)]
    w3blk = [At(f"w3blk{i}", [128, DC * 128], BF16,
                nxt([128, DC * 128], BF16)) for i in range(2)]
    silub = [At(f"silub{i}", [128, 512], BF16, nxt([128, 512], BF16))
             for i in range(2)]
    hidb = [At(f"hidb{i}", [128, 512], BF16, nxt([128, 512], BF16))
            for i in range(2)]
    hid2b = [At(f"hid2b{i}", [128, 512], BF16, nxt([128, 512], BF16))
             for i in range(2)]
    U_END = o
    o = P1
    hsup = At("hsup", [128, NSUP * A * 128], BF16,
              nxt([128, NSUP * A * 128], BF16))
    z2sb = At("z2sb", [128, NSUP * D], F32, nxt([128, NSUP * D], F32))
    W2B = 8
    w2blk = [At(f"w2blk{i}", [128, W2B * 512], BF16,
                nxt([128, W2B * 512], BF16)) for i in range(4)]
    y1in = [At(f"y1in{i}", [128, D], F32, nxt([128, D], F32))
            for i in range(NSUP)]
    outsb = [At(f"outsb{i}", [128, D], F32, nxt([128, D], F32))
             for i in range(2)]
    sqfD = At("sqfD", [128, D], F32, nxt([128, D], F32))
    D_END = o

    ps = [nc.alloc_psum_tensor(f"ps{i}", [128, 512], F32)[:]
          for i in range(8)]

    CVALS = dict(isq=ISQ128, invD=1.0 / D, eps=1e-5, nhalf=-0.5,
                 sq2i=0.70710678118, zero=0.0, one=1.0)
    for i, (k, v) in enumerate(CVALS.items()):
        nc.const_aps.register(F32, float(v), consts[:, i:i + 1])

    P = Prog(nc)
    tq = Sem(nc, "tq")
    vq = Sem(nc, "vq")
    sq = Sem(nc, "sq")
    cs = Sem(nc, "cs")
    xts = [Sem(nc, f"xts{i}") for i in range(2)]
    wqks = [Sem(nc, f"wqks{i}") for i in range(2)]
    wvs = [Sem(nc, f"wvs{i}") for i in range(2)]
    oto = [Sem(nc, f"oto{i}") for i in range(2)]
    pmis = [Sem(nc, f"pmis{i}") for i in range(2)]
    pmos = [Sem(nc, f"pmos{i}") for i in range(2)]
    wos = Sem(nc, "wos")
    xns = [Sem(nc, f"xns{i}") for i in range(2)]
    otis = [Sem(nc, f"otis{i}") for i in range(2)]
    y1to = [Sem(nc, f"y1to{i}") for i in range(2)]
    y1fo = [Sem(nc, f"y1fo{i}") for i in range(2)]
    y1ti = Sem(nc, "y1ti")
    w13s = [Sem(nc, f"w13s{i}") for i in range(2)]
    hido = [Sem(nc, f"hido{i}") for i in range(2)]
    hidi = [Sem(nc, f"hidi{i}") for i in range(2)]
    w2ps = [Sem(nc, f"w2ps{i}") for i in range(4)]
    y1is = [Sem(nc, f"y1is{i}") for i in range(max(2, NSUP))]
    outs = [Sem(nc, f"outs{i}") for i in range(2)]

    def inc(sem, k=1):
        return (sem, k)

    ALLSEMS = ([tq, vq, sq, cs, wos, y1ti] + xts + wqks + wvs + oto +
               pmis + pmos + xns + otis + y1to + y1fo + w13s +
               hido + hidi + w2ps + y1is + outs)

    def phase_barrier():
        toks = [s.tok() for s in ALLSEMS]
        P.emit("v", lambda v: v.memset(nmscal[0:1, 7:8], 0.0),
               waits=toks, incs=[inc(vq)])
        btok = vq.tok()
        for k in list(state.keys()):
            v = state[k]
            if isinstance(v, list):
                state[k] = [btok for _ in v]
            else:
                state[k] = btok
        for i in range(8):
            bank_free[i] = btok
        return btok

    ZT = (tq, 0)
    bank_free = {i: ZT for i in range(8)}
    state = dict(
        xbT_free=ZT, vpair_free=ZT, qk_free=ZT,
        otstage_free=[ZT, ZT],
        wqk_read=[ZT, ZT], wv_read=[ZT, ZT],
        fold_in_free=[ZT, ZT], fold_st_free=[ZT, ZT],
        hid2_free=[ZT, ZT],
        wobuf_free=ZT, xnat_free=[ZT, ZT], ottile_free=[ZT, ZT],
        zsb_free=ZT, sqf_free=ZT, y1f_free=[ZT, ZT], y1fspill=[ZT, ZT],
        y1Tst_free=[ZT, ZT],
        lncol_free=[ZT, ZT], w13_free=[ZT, ZT], silu_free=[ZT, ZT],
        hidb_free=[ZT, ZT], hsup_free=ZT, w2blk_free=[ZT] * 4,
        y1in_free=[ZT] * NSUP, outsb_free=[ZT, ZT], z2_free=ZT,
        bcsb_free=[ZT, ZT],
        ptile_free=[ZT, ZT, ZT, ZT],
    )

    def emit_consts():
        P.emit("g", lambda g: g.dma_start(cosb[:], d_cos), incs=[inc(cs, 16)])
        P.emit("g", lambda g: g.dma_start(sinb[:], d_sin), incs=[inc(cs, 16)])
        P.emit("g", lambda g: g.dma_start(h128b[:], d_h128),
               incs=[inc(cs, 16)])
        P.emit("g", lambda g: g.dma_start(hAb[:], d_hA), incs=[inc(cs, 16)])
        P.emit("g", lambda g: g.dma_start(trib[:], d_tri), incs=[inc(cs, 16)])
        P.emit("g", lambda g: g.dma_start(identb[:], d_id),
               incs=[inc(cs, 16)])
        for i, (k, v) in enumerate(CVALS.items()):
            vv = float(v)
            P.emit("v", lambda v_, vv=vv, i=i:
                   v_.memset(consts[:, i:i + 1], vv))
        P.emit("v", lambda v: v.memset(onesf[:], 1.0))
        P.emit("v", lambda v: v.memset(onesb[:], 1.0), incs=[inc(vq)])

    n_fold = (128 // FP) * (D // FD)

    def emit_fold_chunk(ci):
        P.label = f"fold{ci}"
        par = ci % 2
        pi_ = ci // (D // FD)
        di = ci % (D // FD)
        src = d_w2.rearrange("(a p) d -> a p d", a=A)[
            :, pi_ * FP:(pi_ + 1) * FP, di * FD:(di + 1) * FD]
        P.emit("g", lambda g, src=src, par=par: g.dma_start(
            w2fin[par][:].rearrange("r (p d) -> r p d", p=FP), src),
            waits=[state["fold_in_free"][par]], incs=[inc(pmis[par], 16)])
        ldtok = (pmis[par], pmis[par].n)
        for n0 in range(0, FP * FD, 512):
            b_ = (n0 // 512) % 2
            P.emit("t", lambda t, n0=n0, b_=b_, par=par: t.matmul(
                ps[b_][0:A, 0:512], hAb[0:A, 0:A],
                w2fin[par][:, n0:n0 + 512], start=True, stop=True),
                waits=([ldtok, (cs, cs.n), bank_free[b_]] if n0 < 1024
                       else [bank_free[b_]]),
                incs=[inc(tq)])
            mt = tq.tok()
            P.emit("s", lambda s, n0=n0, b_=b_, par=par: s.copy(
                w2fst[par][:, n0:n0 + 512], ps[b_][0:A, 0:512]),
                waits=([mt, state["fold_st_free"][par]] if n0 == 0
                       else [mt]),
                incs=[inc(sq)])
            bank_free[b_] = sq.tok()
        state["fold_in_free"][par] = tq.tok()
        evtok = sq.tok()
        nwq = FH // (128 * 16)
        hpw = A // nwq
        for w in range(nwq):
            dst = d_w2pp[di, w].rearrange("p (h d) -> h p d", d=FD)[
                :, pi_ * FP:(pi_ + 1) * FP, :]
            src = w2fst[par][w * hpw:(w + 1) * hpw, :].rearrange(
                "r (p d) -> r p d", p=FP)
            P.emit("y", lambda y, dst=dst, src=src: y.dma_start(dst, src),
                   waits=([evtok] if w == 0 else ()),
                   incs=[inc(pmos[par], 16)])
        state["fold_st_free"][par] = (pmos[par], pmos[par].n)

    def emit_xt_load(b):
        for c in range(DC):
            src = d_xT[b, c * 128:(c + 1) * 128, :]
            P.emit("g", lambda g, c=c, src=src:
                   g.dma_start(xbT[:, c * S:(c + 1) * S], src),
                   waits=[state["xbT_free"]] if c == 0 else (),
                   incs=[inc(xts[b], 16)])

    def emit_pair_weight_loads(p, b):
        par = p % 2
        for (dst, src) in ((wqblk[par], d_wq), (wkblk[par], d_wk)):
            ap = src.rearrange("(c r) e -> r c e", r=128)[
                :, :, 2 * p * 128: 2 * p * 128 + 256]
            P.emit("g", lambda g, dst=dst, ap=ap: g.dma_start(
                dst[:].rearrange("r (c e) -> r c e", c=DC), ap),
                   waits=[state["wqk_read"][par]], incs=[inc(wqks[par], 16)])
        ap = d_wv.rearrange("(c r) e -> r c e", r=128)[
            :, :, 2 * p * 128: 2 * p * 128 + 256]
        P.emit("g", lambda g, ap=ap, par=par:
               g.dma_start(wvblk[par][:].rearrange("r (c e) -> r c e", c=DC),
                           ap),
               waits=[state["wv_read"][par]], incs=[inc(wvs[par], 16)])

    def qk_prod(b, wblk, wtok, oute, outo):
        for qc in range(QC):
            n0 = qc * CW
            nn = CW
            be, bo = (qc % 2) * 2, (qc % 2) * 2 + 1
            for eo, bank in ((0, be), (1, bo)):
                for c in range(DC):
                    P.emit("t", lambda t, eo=eo, bank=bank, c=c, n0=n0,
                           nn=nn, wblk=wblk: t.matmul(
                               ps[bank][:, 0:nn],
                               wblk[:, c * 256 + eo * 128:
                                    c * 256 + eo * 128 + 128],
                               xbT[:, c * S + n0:c * S + n0 + nn],
                               start=(c == 0), stop=(c == DC - 1)),
                           waits=([wtok, (xts[b], xts[b].n),
                                   bank_free[bank]] if c == 0 else ()),
                           incs=([inc(tq)] if c == DC - 1 else ()))
            mmtok = tq.tok()
            cS = cosb[:, n0:n0 + nn]
            sS = sinb[:, n0:n0 + nn]
            P.emit("v", lambda v, be=be, nn=nn, cS=cS:
                   v.tensor_tensor(rtmp[0][:, 0:nn], ps[be][:, 0:nn], cS,
                                   ALU.mult),
                   waits=[mmtok, (cs, cs.n), vq.tok()])
            P.emit("v", lambda v, bo=bo, nn=nn, sS=sS:
                   v.tensor_tensor(rtmp[1][:, 0:nn], ps[bo][:, 0:nn], sS,
                                   ALU.mult), incs=[inc(vq)])
            w1_ = vq.tok()
            P.emit("v", lambda v, nn=nn, n0=n0, oute=oute:
                   v.tensor_tensor(oute[:, n0:n0 + nn], rtmp[0][:, 0:nn],
                                   rtmp[1][:, 0:nn], ALU.subtract),
                   waits=[w1_], incs=[inc(vq)])
            subtok = vq.tok()
            P.emit("v", lambda v, be=be, nn=nn, sS=sS:
                   v.tensor_tensor(rtmp[0][:, 0:nn], ps[be][:, 0:nn], sS,
                                   ALU.mult), waits=[subtok])
            P.emit("v", lambda v, bo=bo, nn=nn, cS=cS:
                   v.tensor_tensor(rtmp[1][:, 0:nn], ps[bo][:, 0:nn], cS,
                                   ALU.mult), incs=[inc(vq)])
            w2_ = vq.tok()
            P.emit("v", lambda v, nn=nn, n0=n0, outo=outo:
                   v.tensor_tensor(outo[:, n0:n0 + nn], rtmp[0][:, 0:nn],
                                   rtmp[1][:, 0:nn], ALU.add),
                   waits=[w2_], incs=[inc(vq)])
            done = vq.tok()
            bank_free[be] = done
            bank_free[bo] = done

    def v_prod(b, par):
        wtok = (wvs[par], wvs[par].n)
        war = state["vpair_free"]
        for tc in range(TT):
            bank = 4 + tc % 2
            for c in range(DC):
                P.emit("t", lambda t, tc=tc, c=c, bank=bank, par=par:
                       t.matmul(
                           ps[bank][:, 0:256],
                           xbT[:, c * S + tc * 128:c * S + tc * 128 + 128],
                           wvblk[par][:, c * 256:(c + 1) * 256],
                           start=(c == 0), stop=(c == DC - 1)),
                       waits=([wtok, (xts[b], xts[b].n), bank_free[bank]]
                              if c == 0 else ()),
                       incs=([inc(tq)] if c == DC - 1 else ()))
            mmtok = tq.tok()
            P.emit("s", lambda s, tc=tc, bank=bank:
                   s.copy(vpair[:, tc * 256:(tc + 1) * 256],
                          ps[bank][:, 0:256]),
                   waits=[mmtok, war] if tc == 0 else [mmtok],
                   incs=[inc(sq)])
            bank_free[bank] = sq.tok()

    def emit_attention_pair(p, b, prefetch=None):
        P.label = f"attn_b{b}p{p}"
        par = p % 2
        wtokqk = (wqks[par], wqks[par].n)
        qk_prod(b, wqblk[par], wtokqk, qe, qo)
        qk_prod(b, wkblk[par], wtokqk, ke, ko)
        state["wqk_read"][par] = tq.tok()
        ropetok = vq.tok()
        v_prod(b, par)
        state["wv_read"][par] = tq.tok()
        vtok = sq.tok()
        if prefetch is not None:
            emit_pair_weight_loads(*prefetch)
        ptoks = {}
        rctoks = {}

        def emit_scores(qc, kc):
            q0 = qc * CW
            qn = CW
            k0 = kc * 128
            for hb in (0, 1):
                pb = 64 * hb
                sbank = hb * 2 + kc % 2
                P.emit("t", lambda t, sbank=sbank, pb=pb, k0=k0, q0=q0,
                       qn=qn: t.matmul(
                           ps[sbank][:, 0:qn],
                           ke[pb:pb + 64, k0:k0 + 128],
                           qe[pb:pb + 64, q0:q0 + qn],
                           start=True, stop=False),
                       waits=([bank_free[sbank], ropetok]
                              if (qc == 0 and kc == 0)
                              else [bank_free[sbank]]))
            for hb in (0, 1):
                pb = 64 * hb
                sbank = hb * 2 + kc % 2
                P.emit("t", lambda t, sbank=sbank, pb=pb, k0=k0, q0=q0,
                       qn=qn: t.matmul(
                           ps[sbank][:, 0:qn],
                           ko[pb:pb + 64, k0:k0 + 128],
                           qo[pb:pb + 64, q0:q0 + qn],
                           start=False, stop=True),
                       incs=[inc(tq)])
                smm = tq.tok()
                pi = hb * 2 + kc % 2
                pt = ptile[pi]
                P.emit("s", lambda s, pt=pt, sbank=sbank, qn=qn:
                       s.activation(pt[:, 0:qn], ps[sbank][:, 0:qn],
                                    AF.Exp, scale=ISQ128),
                       waits=[smm, state["ptile_free"][pi]],
                       incs=[inc(sq)])
                exptok = sq.tok()
                bank_free[sbank] = exptok
                ptok = exptok
                qoff = k0 - q0
                if qoff >= 0:
                    if qoff > 0:
                        P.emit("v", lambda v, pt=pt, qoff=qoff:
                               v.memset(pt[:, 0:qoff], 0.0),
                               waits=[exptok])
                    P.emit("v", lambda v, pt=pt, qoff=qoff:
                           v.tensor_tensor(
                               pt[:, qoff:qoff + 128],
                               pt[:, qoff:qoff + 128], trib[:],
                               ALU.mult),
                           waits=([exptok] if qoff == 0 else ()),
                           incs=[inc(vq)])
                    ptok = vq.tok()
                ptoks[(kc, hb)] = ptok

        def emit_sumpv(qc, kc):
            qn = CW
            kcmax = (qc + 1) * KPQ
            last = kc == kcmax - 1
            if last:
                for hb in (0, 1):
                    pi = hb * 2 + kc % 2
                    pt = ptile[pi]
                    rb = 6 + hb
                    P.emit("t", lambda t, rb=rb, pt=pt, qn=qn: t.matmul(
                        ps[rb][0:1, 0:qn], onesb[:, 0:1],
                        pt[:, 0:qn], start=False, stop=True),
                        waits=[ptoks[(kc, hb)]], incs=[inc(tq)])
                    rstok = tq.tok()
                    P.emit("s", lambda s, rb=rb, qn=qn, hb=hb: s.activation(
                        bcsb[hb][0:1, 0:qn], ps[rb][0:1, 0:qn], AF.Ln),
                        waits=[rstok, state["bcsb_free"][hb]],
                        incs=[inc(sq)])
                    P.emit("s", lambda s, qn=qn, hb=hb: s.activation(
                        bcsb[hb][0:1, 0:qn], bcsb[hb][0:1, 0:qn], AF.Exp,
                        scale=-1.0),
                        incs=[inc(sq)])
                    rctoks[hb] = sq.tok()
                for hb in (0, 1):
                    pi = hb * 2 + kc % 2
                    pt = ptile[pi]
                    ob = 4 + hb
                    P.emit("t", lambda t, ob=ob, pt=pt, qn=qn, kc=kc,
                           hb=hb: t.matmul(
                               ps[ob][:, 0:qn],
                               vpair[:, kc * 256 + hb * 128:
                                     kc * 256 + hb * 128 + 128],
                               pt[:, 0:qn], start=False, stop=True),
                           incs=[inc(tq)])
                    state["ptile_free"][pi] = tq.tok()
                return
            for hb in (0, 1):
                pi = hb * 2 + kc % 2
                pt = ptile[pi]
                rb = 6 + hb
                ob = 4 + hb
                P.emit("t", lambda t, rb=rb, pt=pt, qn=qn, kc=kc:
                       t.matmul(
                           ps[rb][0:1, 0:qn], onesb[:, 0:1],
                           pt[:, 0:qn], start=(kc == 0), stop=False),
                       waits=([ptoks[(kc, hb)], bank_free[rb]] if kc == 0
                              else [ptoks[(kc, hb)]]))
                P.emit("t", lambda t, ob=ob, pt=pt, qn=qn, kc=kc, hb=hb:
                       t.matmul(
                           ps[ob][:, 0:qn],
                           vpair[:, kc * 256 + hb * 128:
                                 kc * 256 + hb * 128 + 128],
                           pt[:, 0:qn], start=(kc == 0), stop=False),
                       waits=([vtok, bank_free[ob]] if kc == 0 else ()),
                       incs=[inc(tq)])
                state["ptile_free"][pi] = tq.tok()

        def emit_cluster(qc):
            q0 = qc * CW
            qn = CW
            for hb in (0, 1):
                h = 2 * p + hb
                rb = 6 + hb
                ob = 4 + hb
                ost = otstage[hb]
                P.emit("t", lambda t, rb=rb, qn=qn, hb=hb: t.matmul(
                    ps[rb][:, 0:qn], onesf[:], bcsb[hb][0:1, 0:qn],
                    start=True, stop=True),
                    waits=[rctoks[hb]], incs=[inc(tq)])
                bctok = tq.tok()
                P.emit("s", lambda s, rb=rb, qn=qn, hb=hb: s.copy(
                    bcsb[hb][:, 0:qn], ps[rb][:, 0:qn]),
                    waits=[bctok], incs=[inc(sq)])
                bank_free[rb] = sq.tok()
                bcev = sq.tok()
                P.emit("v", lambda v, ob=ob, q0=q0, qn=qn, ost=ost,
                       hb=hb: v.tensor_tensor(
                           ost[:, q0:q0 + qn], ps[ob][:, 0:qn],
                           bcsb[hb][:, 0:qn], ALU.mult),
                       waits=[bcev] + ([state["otstage_free"][hb]]
                                       if qc == 0 else []),
                       incs=[inc(vq)])
                bank_free[ob] = vq.tok()
                state["bcsb_free"][hb] = vq.tok()
                if qc == QC - 1:
                    oev = vq.tok()
                    P.emit("y", lambda y, h=h, ost=ost, b=b: y.dma_start(
                        d_oT[b, h], ost[:]), waits=[oev],
                        incs=[inc(oto[hb], 16)])
                    state["otstage_free"][hb] = (oto[hb], oto[hb].n)

        tiles = [(qc, kc) for qc in range(QC)
                 for kc in range((qc + 1) * KPQ)]
        prev = None
        pend_cl = None
        for (qc, kc) in tiles:
            emit_scores(qc, kc)
            if prev is not None:
                if pend_cl is not None:
                    emit_cluster(pend_cl)
                    pend_cl = None
                emit_sumpv(*prev)
                if prev[0] != qc:
                    pend_cl = prev[0]
            prev = (qc, kc)
        emit_sumpv(*prev)
        emit_cluster(QC - 1)
        state["qk_free"] = tq.tok()
        state["vpair_free"] = tq.tok()

    def emit_wo_load():
        qtoks = []
        for hq in range(4):
            nh4 = DC // 4
            ap = d_wo.rearrange("(c r) e -> r c e", r=128)[
                :, hq * nh4:(hq + 1) * nh4, :]
            P.emit("g", lambda g, ap=ap, hq=hq, nh4=nh4: g.dma_start(
                wobuf[:, hq * nh4 * D:(hq + 1) * nh4 * D].rearrange(
                    "r (c e) -> r c e", c=nh4), ap),
                waits=([state["wobuf_free"]] if hq == 0 else ()),
                incs=[inc(wos, 16)])
            qtoks.append((wos, wos.n))
        return qtoks

    def emit_ln(zview, sqb, outf_list, ztok, par, extra_war):
        col = lncol[:, par * 4:par * 4 + 4]
        P.emit("v", lambda v: v.reduce_sum(col[:, 0:1], zview, axis=AX.X),
               waits=[ztok, state["lncol_free"][par]], incs=[inc(vq)])
        t1 = vq.tok()
        P.emit("v", lambda v: v.tensor_scalar(
            col[:, 1:2], col[:, 0:1], 1.0 / D, None, ALU.mult),
            waits=[t1], incs=[inc(vq)])
        t2 = vq.tok()
        P.emit("v", lambda v: v.tensor_scalar(
            zview, zview, col[:, 1:2], None, ALU.subtract),
            waits=[t2], incs=[inc(vq)])
        t3 = vq.tok()
        P.emit("s", lambda s: s.activation(sqb[:], zview, AF.Square),
               waits=[t3, state["sqf_free"]], incs=[inc(sq)])
        t4 = sq.tok()
        P.emit("v", lambda v: v.reduce_sum(col[:, 2:3], sqb[:], axis=AX.X),
               waits=[t4], incs=[inc(vq)])
        t5 = vq.tok()
        state["sqf_free"] = t5
        P.emit("s", lambda s: s.activation(
            col[:, 3:4], col[:, 2:3], AF.Ln, scale=CVALS["invD"],
            bias=CVALS["eps"]), waits=[t5], incs=[inc(sq)])
        t6 = sq.tok()
        P.emit("s", lambda s: s.activation(
            col[:, 3:4], col[:, 3:4], AF.Exp, scale=CVALS["nhalf"]),
            waits=[t6], incs=[inc(sq)])
        t7 = sq.tok()
        for outf, war in zip(outf_list, extra_war):
            P.emit("v", lambda v, outf=outf: v.tensor_scalar(
                outf[:], zview, col[:, 3:4], None, ALU.mult),
                waits=[t7] + list(war), incs=[inc(vq)])
        state["lncol_free"][par] = vq.tok()

    pend_tr = []

    def emit_transposes(b, tt, par, lntok, sptok):
        for c in range(DC):
            tb = 4 + c % 4
            P.emit("t", lambda t, c=c, par=par, tb=tb: t.transpose(
                ps[tb][:, 0:128], y1f[par][:, c * 128:(c + 1) * 128],
                identb[:]),
                waits=([lntok, bank_free[tb], (cs, cs.n)] if c < 4
                       else [bank_free[tb]]),
                incs=[inc(tq)])
            trtok = tq.tok()
            if c % 2 == 0:
                P.emit("s", lambda s, c=c, par=par, tb=tb: s.copy(
                    y1Tst[par][:, c * 128:(c + 1) * 128], ps[tb][:, 0:128]),
                    waits=([trtok, state["y1Tst_free"][par]] if c < 2
                           else [trtok]),
                    incs=[inc(sq)])
                bank_free[tb] = sq.tok()
            else:
                P.emit("v", lambda v, c=c, par=par, tb=tb: v.tensor_copy(
                    y1Tst[par][:, c * 128:(c + 1) * 128], ps[tb][:, 0:128]),
                    waits=([trtok, state["y1Tst_free"][par]] if c < 2
                           else [trtok]),
                    incs=[inc(vq)])
                bank_free[tb] = vq.tok()
        trev = [sq.tok(), vq.tok()]
        state["y1f_free"][par] = trev
        dst = d_y1T[:, :, b * S + tt * 128: b * S + (tt + 1) * 128]
        P.emit("g", lambda g, dst=dst, par=par: g.dma_start(
            dst.rearrange("c r e -> r c e"),
            y1Tst[par][:].rearrange("r (c e) -> r c e", c=DC)),
            waits=[trev, sptok], incs=[inc(y1to[par], 16)])
        state["y1Tst_free"][par] = (y1to[par], y1to[par].n)

    def emit_w_in_dma(b, tt):
        par = tt % 2
        P.emit("y", lambda y, b=b, tt=tt, par=par: y.dma_start(
            xnat[par][:], d_xn[b, tt * 128:(tt + 1) * 128, :]),
            waits=[state["xnat_free"][par]], incs=[inc(xns[par], 16)])
        otap = d_oT[b, :, :, tt * 128:(tt + 1) * 128]
        P.emit("y", lambda y, otap=otap, par=par: y.dma_start(
            ottile[par][:].rearrange("r (h e) -> r h e", h=NH),
            otap.rearrange("h r e -> r h e")),
            waits=[state["ottile_free"][par],
                   (oto[0], oto[0].n), (oto[1], oto[1].n)],
            incs=[inc(otis[par], 16)])
        return ((xns[par], xns[par].n), (otis[par], otis[par].n))

    def emit_w_phase(b):
        P.label = f"W{b}"
        woqtoks = emit_wo_load()
        ndc = D // 512
        toks = emit_w_in_dma(b, 0)
        for tt in range(TT):
            par = tt % 2
            P.label = f"W{b}t{tt}"
            xtok, ottok = toks
            for h in range(NH):
                for dc in range(ndc):
                    P.emit("t", lambda t, h=h, dc=dc, par=par: t.matmul(
                        ps[dc][:, :],
                        ottile[par][:, h * 128:(h + 1) * 128],
                        wobuf[:, h * D + dc * 512: h * D + dc * 512 + 512],
                        start=(h == 0), stop=(h == NH - 1)),
                        waits=(([ottok] +
                                [bank_free[i] for i in range(ndc)])
                               if (h == 0 and dc == 0) else []) +
                        ([woqtoks[h // (NH // 4)]]
                         if (h % (NH // 4) == 0 and dc == 0) else []),
                        incs=([inc(tq)]
                              if (h == NH - 1 and dc == ndc - 1) else ()))
            wtok = tq.tok()
            state["ottile_free"][par] = wtok
            if tt + 1 < TT:
                toks = emit_w_in_dma(b, tt + 1)
            if pend_tr:
                emit_transposes(*pend_tr.pop(0))
            for dc in range(ndc):
                P.emit("v", lambda v, dc=dc, par=par: v.tensor_tensor(
                    zsb[:, dc * 512:(dc + 1) * 512], ps[dc][:, :],
                    xnat[par][:, dc * 512:(dc + 1) * 512], ALU.add),
                    waits=([wtok, xtok, state["zsb_free"]] if dc == 0
                           else []),
                    incs=([inc(vq)] if dc == ndc - 1 else ()))
            ztok = vq.tok()
            for dc in range(ndc):
                bank_free[dc] = ztok
            state["xnat_free"][par] = ztok
            emit_ln(zsb[:], sqf, [y1f[par]], ztok, par,
                    [[state["y1f_free"][par], state["y1fspill"][par]]])
            lntok = vq.tok()
            state["zsb_free"] = lntok
            P.emit("g", lambda g, b=b, tt=tt, par=par: g.dma_start(
                d_y1f[b * S + tt * 128: b * S + (tt + 1) * 128, :],
                y1f[par][:]),
                waits=[lntok], incs=[inc(y1fo[par], 16)])
            sptok = (y1fo[par], y1fo[par].n)
            state["y1fspill"][par] = sptok
            pend_tr.append((b, tt, par, lntok, sptok))
        while pend_tr:
            emit_transposes(*pend_tr.pop(0))
        state["wobuf_free"] = tq.tok()

    def emit_u_phase():
        P.label = "U"
        uw = [(y1to[0], y1to[0].n), (y1to[1], y1to[1].n),
              state["xbT_free"], state["wobuf_free"]]
        for c in range(DC):
            P.emit("y" if c % 2 == 0 else "g",
                   lambda e, c=c: e.dma_start(
                y1T[:, c * B * S:(c + 1) * B * S], d_y1T[c]),
                waits=(uw if c < 2 else ()), incs=[inc(y1ti, 16)])
        y1ttok = (y1ti, y1ti.n)
        ntc = B * S // 512
        pend_h = []

        def emit_h128(hc, tc, multok):
            tb = 4 + tc % 2
            hb_ = hidb[tc % 2]
            P.emit("t", lambda t, hb_=hb_, tb=tb: t.matmul(
                ps[tb][:, :], h128b[:], hb_[:], start=True, stop=True),
                waits=[multok, bank_free[tb]], incs=[inc(tq)])
            htok = tq.tok()
            state["hidb_free"][tc % 2] = htok
            h2 = hid2b[tc % 2]
            P.emit("s", lambda s, h2=h2, tb=tb: s.copy(
                h2[:], ps[tb][:, :]),
                waits=[htok, state["hid2_free"][tc % 2]],
                incs=[inc(sq)])
            bank_free[tb] = sq.tok()
            h2ev = sq.tok()
            dsth = d_hid[:, hc, tc * 512:(tc + 1) * 512]
            P.emit("y", lambda y, dsth=dsth, h2=h2: y.dma_start(
                dsth, h2[:]),
                waits=[h2ev], incs=[inc(hido[tc % 2], 16)])
            state["hid2_free"][tc % 2] = (hido[tc % 2],
                                          hido[tc % 2].n)

        for hc in range(A):
            par = hc % 2
            for (dst, src, war) in ((w1blk[par], d_w1,
                                     state["w13_free"][par]),
                                    (w3blk[par], d_w3, ZT)):
                ap = src.rearrange("(c r) e -> r c e", r=128)[
                    :, :, hc * 128:(hc + 1) * 128]
                P.emit("g", lambda g, dst=dst, ap=ap: g.dma_start(
                    dst[:].rearrange("r (c e) -> r c e", c=DC), ap),
                       waits=[war], incs=[inc(w13s[par], 16)])
            wtok = (w13s[par], w13s[par].n)
            for tc in range(ntc):
                gb = tc % 2
                lb = 2 + tc % 2
                for c in range(DC):
                    P.emit("t", lambda t, c=c, gb=gb, par=par, tc=tc:
                           t.matmul(ps[gb][:, :],
                                    w1blk[par][:, c * 128:(c + 1) * 128],
                                    y1T[:, c * B * S + tc * 512:
                                        c * B * S + tc * 512 + 512],
                                    start=(c == 0), stop=(c == DC - 1)),
                           waits=([wtok, y1ttok, bank_free[gb]]
                                  if c == 0 else ()))
                if pend_h:
                    emit_h128(*pend_h.pop(0))
                for c in range(DC):
                    P.emit("t", lambda t, c=c, lb=lb, par=par, tc=tc:
                           t.matmul(ps[lb][:, :],
                                    w3blk[par][:, c * 128:(c + 1) * 128],
                                    y1T[:, c * B * S + tc * 512:
                                        c * B * S + tc * 512 + 512],
                                    start=(c == 0), stop=(c == DC - 1)),
                           waits=([bank_free[lb]] if c == 0 else ()),
                           incs=([inc(tq)] if c == DC - 1 else ()))
                mmtok = tq.tok()
                sb_ = silub[tc % 2]
                hb_ = hidb[tc % 2]
                P.emit("s", lambda s, gb=gb, sb_=sb_: s.activation(
                    sb_[:], ps[gb][:, :], AF.Silu),
                    waits=[mmtok, state["silu_free"][tc % 2]],
                    incs=[inc(sq)])
                silutok = sq.tok()
                bank_free[gb] = silutok
                P.emit("v", lambda v, lb=lb, sb_=sb_, hb_=hb_:
                       v.tensor_tensor(hb_[:], sb_[:], ps[lb][:, :],
                                       ALU.mult),
                       waits=[silutok, state["hidb_free"][tc % 2]],
                       incs=[inc(vq)])
                multok = vq.tok()
                bank_free[lb] = multok
                state["silu_free"][tc % 2] = multok
                pend_h.append((hc, tc, multok))
            state["w13_free"][par] = tq.tok()
        while pend_h:
            emit_h128(*pend_h.pop(0))

    def emit_d_phase():
        P.label = "D"
        NSUPS = B * TT // NSUP
        hid_all = [(hido[0], hido[0].n), (hido[1], hido[1].n)]
        pm_all = [(pmos[0], pmos[0].n), (pmos[1], pmos[1].n)]
        y1f_all = [(y1fo[0], y1fo[0].n), (y1fo[1], y1fo[1].n)]
        ndc4 = D // 512
        nblk = A // W2B
        for sup in range(NSUPS):
            spar = sup % 2
            for half in range(2):
                a0 = half * (A // 2)
                src = d_hid[:, a0:a0 + A // 2,
                            sup * NSUP * 128:(sup + 1) * NSUP * 128]
                P.emit("g" if half == 0 else "y",
                       lambda e, src=src, a0=a0: e.dma_start(
                           hsup[:, a0 * 512:(a0 + A // 2) * 512].rearrange(
                               "r (a e) -> r a e", a=A // 2), src),
                       waits=(hid_all + [state["hsup_free"]]),
                       incs=[inc(hidi[spar], 16)])
            for j in range(NSUP):
                tt = sup * NSUP + j
                P.emit("y", lambda y, tt=tt, j=j: y.dma_start(
                    y1in[j][:], d_y1f[tt * 128:(tt + 1) * 128, :]),
                    waits=[state["y1in_free"][j]] + (y1f_all
                                                     if sup == 0 else []),
                    incs=[inc(y1is[j], 16)])
            htok = (hidi[spar], hidi[spar].n)
            ytoks = [(y1is[j], y1is[j].n) for j in range(NSUP)]
            for dc4 in range(ndc4):
                for bi in range(nblk):
                    wb = bi % 4
                    ap = d_w2pp[dc4, bi // 2, :,
                                (bi % 2) * W2B * 512:
                                (bi % 2 + 1) * W2B * 512]
                    P.emit(("y", "g", "s", "y")[bi % 4],
                           lambda e, ap=ap, wb=wb: e.dma_start(
                        w2blk[wb][:], ap),
                        waits=[state["w2blk_free"][wb]] +
                        (pm_all if sup == 0 and dc4 == 0 else []),
                        incs=[inc(w2ps[wb], 16)])
                    wtok = (w2ps[wb], w2ps[wb].n)
                    for hoff in range(W2B):
                        hc = bi * W2B + hoff
                        for j in range(NSUP):
                            P.emit("t", lambda t, hc=hc, j=j, hoff=hoff,
                                   wb=wb: t.matmul(
                                       ps[j][:, :],
                                       hsup[:, hc * 512 + j * 128:
                                            hc * 512 + j * 128 + 128],
                                       w2blk[wb][:, hoff * 512:
                                                 hoff * 512 + 512],
                                       start=(hc == 0), stop=(hc == A - 1)),
                                   waits=([htok, wtok] +
                                          [bank_free[i]
                                           for i in range(NSUP)])
                                   if hc == 0 and j == 0 else
                                   ([wtok] if hoff == 0 and j == 0
                                    else ()),
                                   incs=([inc(tq)]
                                         if (hoff == W2B - 1
                                             and j == NSUP - 1)
                                         else ()))
                    state["w2blk_free"][wb] = tq.tok()
                mmtok = tq.tok()
                for j in range(NSUP):
                    tt = sup * NSUP + j
                    P.emit("v", lambda v, j=j, dc4=dc4:
                           v.tensor_tensor(
                               z2sb[:, j * D + dc4 * 512:
                                    j * D + dc4 * 512 + 512],
                               ps[j][:, :],
                               y1in[j][:, dc4 * 512:(dc4 + 1) * 512],
                               ALU.add),
                           waits=[mmtok, ytoks[j]] +
                           ([state["z2_free"]] if j == 0 and dc4 == 0
                            else []),
                           incs=[inc(vq)])
                    bank_free[j] = vq.tok()
            z2tok = vq.tok()
            for j in range(NSUP):
                state["y1in_free"][j] = z2tok
                tt = sup * NSUP + j
                b = tt // TT
                trow = tt % TT
                par = tt % 2
                zview = z2sb[:, j * D:(j + 1) * D]
                emit_ln(zview, sqfD, [outsb[par]], z2tok, par,
                        [[state["outsb_free"][par]]])
                otok = vq.tok()
                P.emit("y", lambda y, b=b, trow=trow, par=par: y.dma_start(
                    d_y[b, trow * 128:(trow + 1) * 128, :], outsb[par][:]),
                    waits=[otok], incs=[inc(outs[par], 16)])
                state["outsb_free"][par] = (outs[par], outs[par].n)
            state["z2_free"] = vq.tok()
            state["hsup_free"] = tq.tok()

    emit_consts()
    pm_per_pair = max(1, (n_fold + B * PAIRS - 1) // (B * PAIRS))
    pm_next = [0]
    for b in range(B):
        emit_pair_weight_loads(0, b)
        emit_xt_load(b)
        for p in range(PAIRS):
            for _ in range(pm_per_pair):
                if pm_next[0] < n_fold:
                    emit_fold_chunk(pm_next[0])
                    pm_next[0] += 1
            nxt = (p + 1, b) if p + 1 < PAIRS else None
            emit_attention_pair(p, b, nxt)
        while b == B - 1 and pm_next[0] < n_fold:
            emit_fold_chunk(pm_next[0])
            pm_next[0] += 1
        phase_barrier()
        emit_w_phase(b)
        state["xbT_free"] = tq.tok()
        if b < B - 1:
            phase_barrier()
    phase_barrier()
    emit_u_phase()
    phase_barrier()
    emit_d_phase()
    fin = [(outs[0], outs[0].n), (outs[1], outs[1].n)]
    P.emit("y", lambda y: None, waits=fin)
    P.run()
    return nc


def _hadamard(n):
    H = np.array([[1.0]], dtype=np.float32)
    while H.shape[0] < n:
        H = np.block([[H, H], [H, -H]])
    return H


def host_consts(cfg: Cfg):
    S, A = cfg.S, cfg.A
    inv_freq = 1.0 / (10000.0 ** (np.arange(0, 128, 2, dtype=np.float32)
                                  / 128.0))
    t = np.arange(S, dtype=np.float32)
    fr = np.outer(inv_freq, t)
    cos = np.cos(fr).astype(np.float32)
    sin = np.sin(fr).astype(np.float32)
    cosd = np.concatenate([cos, cos], axis=0)
    sind = np.concatenate([sin, sin], axis=0)
    h128c = (_hadamard(128) * (2.0 ** -3.5)).astype(np.float32)
    AA = min(A, 64)
    hA = (_hadamard(AA) * (2.0 ** -3)).astype(np.float32)
    hAc = np.zeros((128, AA), dtype=np.float32)
    hAc[0:AA] = hA
    if A >= 64:
        hAc[64:128] = hA
    identc = np.eye(128, dtype=np.float32)
    tric = np.triu(np.ones((128, 128), dtype=np.float32))
    return dict(cosd=cosd, sind=sind, h128c=h128c, hAc=hAc, identc=identc,
                tric=tric)


def qk_perm(cfg: Cfg):
    perm = np.zeros(cfg.D, dtype=np.int64)
    for p in range(cfg.PAIRS):
        for eo in range(2):
            ci = 2 * p + eo
            for hb in range(2):
                h = 2 * p + hb
                for i in range(64):
                    perm[ci * 128 + hb * 64 + i] = h * 128 + 2 * i + eo
    return perm


def kernel(L0_attn_x, L0_attn_Wq, L0_attn_Wk, L0_attn_Wv, L0_attn_Wo,
           L0_ffn_W1, L0_ffn_W2, L0_ffn_W3):
    cfg = Cfg()
    x = np.ascontiguousarray(np.asarray(L0_attn_x, dtype=np.float32))
    Wq = np.asarray(L0_attn_Wq, np.float32)
    Wk = np.asarray(L0_attn_Wk, np.float32)
    Wv = np.asarray(L0_attn_Wv, np.float32)
    Wo = np.asarray(L0_attn_Wo, np.float32)
    W1 = np.asarray(L0_ffn_W1, np.float32)
    W2 = np.asarray(L0_ffn_W2, np.float32)
    W3 = np.asarray(L0_ffn_W3, np.float32)

    qperm = qk_perm(cfg)
    shared = host_consts(cfg)
    shared["wqT_p"] = np.ascontiguousarray(Wq.T[:, qperm])
    shared["wkT_p"] = np.ascontiguousarray(Wk.T[:, qperm])
    shared["wvT"] = np.ascontiguousarray(Wv.T)
    shared["woT"] = np.ascontiguousarray(Wo.T)
    shared["w1T_p"] = np.ascontiguousarray(W1.T)
    shared["w3T_p"] = np.ascontiguousarray(W3.T)
    shared["w2T"] = np.ascontiguousarray(W2.T)

    nc = build(cfg)
    in_maps = []
    for c in range(cfg.NCORES):
        d = dict(shared)
        xs = x[c * cfg.B:(c + 1) * cfg.B]
        d["xT"] = np.ascontiguousarray(xs.transpose(0, 2, 1))
        d["x_nat"] = np.ascontiguousarray(xs)
        in_maps.append(d)
    res = run_bass_kernel_spmd(nc, in_maps, core_ids=list(range(cfg.NCORES)))
    out = np.concatenate([np.asarray(r["y"], np.float32)
                          for r in res.results], axis=0)
    return out


# version 14
# speedup vs baseline: 1.1020x; 1.0257x over previous
import math
import numpy as np
import concourse.bass as bass
import concourse.mybir as mybir
from concourse.bass_utils import run_bass_kernel_spmd

F32 = mybir.dt.float32
BF16 = mybir.dt.bfloat16
AF = mybir.ActivationFunctionType
ALU = mybir.AluOpType
AX = mybir.AxisListType

ISQ128 = float(1.0 / math.sqrt(128.0))


class Cfg:
    def __init__(self, S=2048, NH=16, FH=8192, B=2, NCORES=8):
        self.S = S
        self.NH = NH
        self.D = NH * 128
        self.FH = FH
        self.B = B
        self.NCORES = NCORES
        self.PAIRS = NH // 2
        self.DC = self.D // 128
        self.A = FH // 128
        self.QC = max(1, S // 512)
        self.TT = S // 128
        self.KPQ = (S // self.QC) // 128
        self.NSUP = 4 if (B * self.TT) % 4 == 0 else 1


class Sem:
    __slots__ = ("h", "n", "name")

    def __init__(self, nc, name):
        self.h = nc.alloc_semaphore(name)
        self.n = 0
        self.name = name

    def tok(self):
        return (self, self.n)


class Prog:
    ENGS = ("t", "v", "s", "g", "y")

    def __init__(self, nc):
        self.nc = nc
        self.q = {e: [] for e in self.ENGS}
        self.label = ""
        self.hwm = {e: {} for e in self.ENGS}

    def emit(self, e, fn, waits=(), incs=()):
        ws = []
        hw = self.hwm[e]

        def _flat(items):
            for w in items:
                if w is None:
                    continue
                if isinstance(w, list):
                    _flat(w)
                elif w[1] > hw.get(w[0], 0):
                    hw[w[0]] = w[1]
                    ws.append(w)
        _flat(waits)
        for s, k in incs:
            s.n += k
        self.q[e].append((fn, tuple(ws), tuple(incs), self.label))

    def run(self):
        block_map = {"t": "tensor", "v": "vector", "s": "scalar",
                     "g": "gpsimd", "y": "sync"}
        with self.nc.Block() as block:
            for e in self.ENGS:
                ops = self.q[e]

                def make_runner(ops):
                    def _(eng):
                        for fn, ws, incs, lbl in ops:
                            for s, v in ws:
                                eng.wait_ge(s.h, v)
                            inst = fn(eng)
                            if inst is not None and lbl:
                                inst.annotate(lbl)
                            for s, k in incs:
                                inst = inst.then_inc(s.h, k)
                    return _
                getattr(block, block_map[e])(make_runner(ops))


def build(cfg: Cfg):
    nc = bass.Bass("TRN2", target_bir_lowering=False)
    S, D, NH, FH, B = cfg.S, cfg.D, cfg.NH, cfg.FH, cfg.B
    DC, A, QC, TT, KPQ = cfg.DC, cfg.A, cfg.QC, cfg.TT, cfg.KPQ
    PAIRS, NSUP = cfg.PAIRS, cfg.NSUP
    NCH = S // 512 if S >= 512 else 1
    CW = min(512, S)

    d_xT = nc.dram_tensor("xT", [B, D, S], F32, kind="ExternalInput").ap()
    d_xn = nc.dram_tensor("x_nat", [B, S, D], F32, kind="ExternalInput").ap()
    d_wq = nc.dram_tensor("wqT_p", [D, D], F32, kind="ExternalInput").ap()
    d_wk = nc.dram_tensor("wkT_p", [D, D], F32, kind="ExternalInput").ap()
    d_wv = nc.dram_tensor("wvT", [D, D], F32, kind="ExternalInput").ap()
    d_wo = nc.dram_tensor("woT", [D, D], F32, kind="ExternalInput").ap()
    d_w1 = nc.dram_tensor("w1T_p", [D, FH], F32, kind="ExternalInput").ap()
    d_w3 = nc.dram_tensor("w3T_p", [D, FH], F32, kind="ExternalInput").ap()
    d_w2 = nc.dram_tensor("w2T", [FH, D], F32, kind="ExternalInput").ap()
    d_cos = nc.dram_tensor("cosd", [128, S], F32, kind="ExternalInput").ap()
    d_sin = nc.dram_tensor("sind", [128, S], F32, kind="ExternalInput").ap()
    d_h128 = nc.dram_tensor("h128c", [128, 128], F32, kind="ExternalInput").ap()
    d_hA = nc.dram_tensor("hAc", [128, min(A, 64)], F32,
                          kind="ExternalInput").ap()
    d_id = nc.dram_tensor("identc", [128, 128], F32, kind="ExternalInput").ap()
    d_tri = nc.dram_tensor("tric", [128, 128], F32, kind="ExternalInput").ap()
    d_y = nc.dram_tensor("y", [B, S, D], F32, kind="ExternalOutput").ap()
    d_w2pp = nc.dram_tensor("w2pp_s", [D // 512, FH // (128 * 16), 128,
                                       16 * 512], BF16, kind="Internal").ap()
    d_hid = nc.dram_tensor("hid_s", [128, A, B * S], BF16,
                           kind="Internal").ap()
    d_y1T = nc.dram_tensor("y1T_s", [DC, 128, B * S], BF16,
                           kind="Internal").ap()
    d_y1f = nc.dram_tensor("y1f_s", [B * S, D], F32, kind="Internal").ap()
    d_oT = nc.dram_tensor("oT_s", [B, NH, 128, S], BF16, kind="Internal").ap()

    SBF = {}
    OFFS = {}
    _off = [16640]

    def At(name, shape, dtype, off=None):
        if off is None:
            off = _off[0]
            per = int(np.prod(shape[1:])) * (2 if dtype == BF16 else 4)
            _off[0] = off + ((per + 31) // 32) * 32
        t = nc.alloc_sbuf_tensor_at(name, shape, dtype, offset=off)
        SBF[name] = t[:]
        OFFS[name] = off
        return t[:]

    def size_of(shape, dtype):
        return int(np.prod(shape[1:])) * (2 if dtype == BF16 else 4)

    consts = At("consts", [128, 16], F32)
    onesf = At("onesf", [1, 128], F32)
    onesb = At("onesb", [128, 128], BF16)
    h128b = At("h128b", [128, 128], BF16)
    hAb = At("hAb", [128, min(A, 64)], BF16)
    identb = At("identf", [128, 128], F32)
    trib = At("trib", [128, 128], BF16)
    lncol = At("lncol", [128, 8], F32)
    nmscal = At("nmscal", [1, 8], F32)
    cosb = At("cosb", [128, S], BF16)
    sinb = At("sinb", [128, S], BF16)
    PBASE = ((_off[0] + 63) // 64) * 64

    P1 = PBASE
    P1_SZ = size_of([128, DC * S], BF16)
    P2 = P1 + P1_SZ
    o = P2
    def nxt(shape, dtype):
        nonlocal o
        off = o
        o += ((size_of(shape, dtype) + 31) // 32) * 32
        return off
    xbT = At("xbT", [128, DC * S], BF16, P1)
    qe = At("qe", [128, S], BF16, nxt([128, S], BF16))
    qo = At("qo", [128, S], BF16, nxt([128, S], BF16))
    ke = At("ke", [128, S], BF16, nxt([128, S], BF16))
    ko = At("ko", [128, S], BF16, nxt([128, S], BF16))
    vpair = At("vpair", [128, TT * 256], BF16, nxt([128, TT * 256], BF16))
    wqblk = [At(f"wqblk{i}", [128, DC * 256], BF16,
                nxt([128, DC * 256], BF16)) for i in range(2)]
    wkblk = [At(f"wkblk{i}", [128, DC * 256], BF16,
                nxt([128, DC * 256], BF16)) for i in range(2)]
    wvblk = [At(f"wvblk{i}", [128, DC * 256], BF16,
                nxt([128, DC * 256], BF16)) for i in range(2)]
    ptile = [At(f"ptile{i}", [128, CW], BF16, nxt([128, CW], BF16))
             for i in range(4)]
    rtmp = [At(f"rtmp{i}", [128, CW], F32, nxt([128, CW], F32))
            for i in range(2)]
    bcsb = [At(f"bcsb{i}", [128, CW], F32, nxt([128, CW], F32))
            for i in range(2)]
    otstage = [At(f"otstage{i}", [128, S], BF16, nxt([128, S], BF16))
               for i in range(2)]
    FP = 8
    FD = 512
    w2fin = [At(f"w2fin{i}", [A, FP * FD], BF16, nxt([A, FP * FD], BF16))
             for i in range(2)]
    w2fst = [At(f"w2fst{i}", [A, FP * FD], BF16, nxt([A, FP * FD], BF16))
             for i in range(2)]
    A_END = o
    o = P2
    wobuf = At("wobuf", [128, DC * D], BF16, nxt([128, DC * D], BF16))
    xnat = [At(f"xnat{i}", [128, D], F32, nxt([128, D], F32))
            for i in range(2)]
    ottile = [At(f"ottile{i}", [128, NH * 128], BF16,
                 nxt([128, NH * 128], BF16)) for i in range(2)]
    zsb = At("zsb", [128, D], F32, nxt([128, D], F32))
    sqf = At("sqf", [128, D], F32, nxt([128, D], F32))
    y1f = [At(f"y1f{i}", [128, D], F32, nxt([128, D], F32))
           for i in range(2)]
    y1Tst = [At(f"y1Tst{i}", [128, DC * 128], BF16,
                nxt([128, DC * 128], BF16)) for i in range(2)]
    W_END = o
    y1T = At("y1T", [128, DC * B * S], BF16, P1)
    o = P1 + size_of([128, DC * B * S], BF16)
    w1blk = [At(f"w1blk{i}", [128, DC * 128], BF16,
                nxt([128, DC * 128], BF16)) for i in range(2)]
    w3blk = [At(f"w3blk{i}", [128, DC * 128], BF16,
                nxt([128, DC * 128], BF16)) for i in range(2)]
    silub = [At(f"silub{i}", [128, 512], BF16, nxt([128, 512], BF16))
             for i in range(2)]
    hidb = [At(f"hidb{i}", [128, 512], BF16, nxt([128, 512], BF16))
            for i in range(2)]
    hid2b = [At(f"hid2b{i}", [128, 512], BF16, nxt([128, 512], BF16))
             for i in range(2)]
    U_END = o
    o = P1
    hsup = At("hsup", [128, NSUP * A * 128], BF16,
              nxt([128, NSUP * A * 128], BF16))
    z2sb = At("z2sb", [128, NSUP * D], F32, nxt([128, NSUP * D], F32))
    W2B = 8
    w2blk = [At(f"w2blk{i}", [128, W2B * 512], BF16,
                nxt([128, W2B * 512], BF16)) for i in range(4)]
    y1in = [At(f"y1in{i}", [128, D], F32, nxt([128, D], F32))
            for i in range(NSUP)]
    outsb = [At(f"outsb{i}", [128, D], F32, nxt([128, D], F32))
             for i in range(2)]
    sqfD = At("sqfD", [128, D], F32, nxt([128, D], F32))
    D_END = o

    ps = [nc.alloc_psum_tensor(f"ps{i}", [128, 512], F32)[:]
          for i in range(8)]

    CVALS = dict(isq=ISQ128, invD=1.0 / D, eps=1e-5, nhalf=-0.5,
                 sq2i=0.70710678118, zero=0.0, one=1.0)
    for i, (k, v) in enumerate(CVALS.items()):
        nc.const_aps.register(F32, float(v), consts[:, i:i + 1])

    P = Prog(nc)
    tq = Sem(nc, "tq")
    vq = Sem(nc, "vq")
    sq = Sem(nc, "sq")
    cs = Sem(nc, "cs")
    xts = [Sem(nc, f"xts{i}") for i in range(2)]
    wqks = [Sem(nc, f"wqks{i}") for i in range(2)]
    wvs = [Sem(nc, f"wvs{i}") for i in range(2)]
    oto = [Sem(nc, f"oto{i}") for i in range(2)]
    pmis = [Sem(nc, f"pmis{i}") for i in range(2)]
    pmos = [Sem(nc, f"pmos{i}") for i in range(2)]
    wos = Sem(nc, "wos")
    xns = [Sem(nc, f"xns{i}") for i in range(2)]
    otis = [Sem(nc, f"otis{i}") for i in range(2)]
    y1to = [Sem(nc, f"y1to{i}") for i in range(2)]
    y1fo = [Sem(nc, f"y1fo{i}") for i in range(2)]
    y1ti = Sem(nc, "y1ti")
    w13s = [Sem(nc, f"w13s{i}") for i in range(2)]
    hido = [Sem(nc, f"hido{i}") for i in range(2)]
    hidi = [Sem(nc, f"hidi{i}") for i in range(2)]
    w2ps = [Sem(nc, f"w2ps{i}") for i in range(4)]
    y1is = [Sem(nc, f"y1is{i}") for i in range(max(2, NSUP))]
    outs = [Sem(nc, f"outs{i}") for i in range(2)]

    def inc(sem, k=1):
        return (sem, k)

    ALLSEMS = ([tq, vq, sq, cs, wos, y1ti] + xts + wqks + wvs + oto +
               pmis + pmos + xns + otis + y1to + y1fo + w13s +
               hido + hidi + w2ps + y1is + outs)

    def phase_barrier():
        toks = [s.tok() for s in ALLSEMS]
        P.emit("v", lambda v: v.memset(nmscal[0:1, 7:8], 0.0),
               waits=toks, incs=[inc(vq)])
        btok = vq.tok()
        for k in list(state.keys()):
            v = state[k]
            if isinstance(v, list):
                state[k] = [btok for _ in v]
            else:
                state[k] = btok
        for i in range(8):
            bank_free[i] = btok
        return btok

    ZT = (tq, 0)
    bank_free = {i: ZT for i in range(8)}
    state = dict(
        xbT_free=ZT, vpair_free=ZT, qk_free=ZT,
        otstage_free=[ZT, ZT],
        wqk_read=[ZT, ZT], wv_read=[ZT, ZT],
        fold_in_free=[ZT, ZT], fold_st_free=[ZT, ZT],
        hid2_free=[ZT, ZT],
        wobuf_free=ZT, xnat_free=[ZT, ZT], ottile_free=[ZT, ZT],
        zsb_free=ZT, sqf_free=ZT, y1f_free=[ZT, ZT], y1fspill=[ZT, ZT],
        y1Tst_free=[ZT, ZT],
        lncol_free=[ZT, ZT], w13_free=[ZT, ZT], silu_free=[ZT, ZT],
        hidb_free=[ZT, ZT], hsup_free=ZT, w2blk_free=[ZT] * 4,
        y1in_free=[ZT] * NSUP, outsb_free=[ZT, ZT], z2_free=ZT,
        bcsb_free=[ZT, ZT],
        ptile_free=[ZT, ZT, ZT, ZT],
    )

    def emit_consts():
        P.emit("g", lambda g: g.dma_start(cosb[:], d_cos), incs=[inc(cs, 16)])
        P.emit("g", lambda g: g.dma_start(sinb[:], d_sin), incs=[inc(cs, 16)])
        P.emit("g", lambda g: g.dma_start(h128b[:], d_h128),
               incs=[inc(cs, 16)])
        P.emit("g", lambda g: g.dma_start(hAb[:], d_hA), incs=[inc(cs, 16)])
        P.emit("g", lambda g: g.dma_start(trib[:], d_tri), incs=[inc(cs, 16)])
        P.emit("g", lambda g: g.dma_start(identb[:], d_id),
               incs=[inc(cs, 16)])
        for i, (k, v) in enumerate(CVALS.items()):
            vv = float(v)
            P.emit("v", lambda v_, vv=vv, i=i:
                   v_.memset(consts[:, i:i + 1], vv))
        P.emit("v", lambda v: v.memset(onesf[:], 1.0))
        P.emit("v", lambda v: v.memset(onesb[:], 1.0), incs=[inc(vq)])

    n_fold = (128 // FP) * (D // FD)

    def emit_fold_chunk(ci):
        P.label = f"fold{ci}"
        par = ci % 2
        pi_ = ci // (D // FD)
        di = ci % (D // FD)
        src = d_w2.rearrange("(a p) d -> a p d", a=A)[
            :, pi_ * FP:(pi_ + 1) * FP, di * FD:(di + 1) * FD]
        P.emit("g", lambda g, src=src, par=par: g.dma_start(
            w2fin[par][:].rearrange("r (p d) -> r p d", p=FP), src),
            waits=[state["fold_in_free"][par]], incs=[inc(pmis[par], 16)])
        ldtok = (pmis[par], pmis[par].n)
        for n0 in range(0, FP * FD, 512):
            b_ = (n0 // 512) % 2
            P.emit("t", lambda t, n0=n0, b_=b_, par=par: t.matmul(
                ps[b_][0:A, 0:512], hAb[0:A, 0:A],
                w2fin[par][:, n0:n0 + 512], start=True, stop=True),
                waits=([ldtok, (cs, cs.n), bank_free[b_]] if n0 < 1024
                       else [bank_free[b_]]),
                incs=[inc(tq)])
            mt = tq.tok()
            P.emit("s", lambda s, n0=n0, b_=b_, par=par: s.copy(
                w2fst[par][:, n0:n0 + 512], ps[b_][0:A, 0:512]),
                waits=([mt, state["fold_st_free"][par]] if n0 == 0
                       else [mt]),
                incs=[inc(sq)])
            bank_free[b_] = sq.tok()
        state["fold_in_free"][par] = tq.tok()
        evtok = sq.tok()
        nwq = FH // (128 * 16)
        hpw = A // nwq
        for w in range(nwq):
            dst = d_w2pp[di, w].rearrange("p (h d) -> h p d", d=FD)[
                :, pi_ * FP:(pi_ + 1) * FP, :]
            src = w2fst[par][w * hpw:(w + 1) * hpw, :].rearrange(
                "r (p d) -> r p d", p=FP)
            P.emit("y", lambda y, dst=dst, src=src: y.dma_start(dst, src),
                   waits=([evtok] if w == 0 else ()),
                   incs=[inc(pmos[par], 16)])
        state["fold_st_free"][par] = (pmos[par], pmos[par].n)

    def emit_xt_load(b):
        for c in range(DC):
            src = d_xT[b, c * 128:(c + 1) * 128, :]
            P.emit("g", lambda g, c=c, src=src:
                   g.dma_start(xbT[:, c * S:(c + 1) * S], src),
                   waits=[state["xbT_free"]] if c == 0 else (),
                   incs=[inc(xts[b], 16)])

    def emit_pair_weight_loads(p, b):
        par = p % 2
        for (dst, src) in ((wqblk[par], d_wq), (wkblk[par], d_wk)):
            ap = src.rearrange("(c r) e -> r c e", r=128)[
                :, :, 2 * p * 128: 2 * p * 128 + 256]
            P.emit("g", lambda g, dst=dst, ap=ap: g.dma_start(
                dst[:].rearrange("r (c e) -> r c e", c=DC), ap),
                   waits=[state["wqk_read"][par]], incs=[inc(wqks[par], 16)])
        ap = d_wv.rearrange("(c r) e -> r c e", r=128)[
            :, :, 2 * p * 128: 2 * p * 128 + 256]
        P.emit("g", lambda g, ap=ap, par=par:
               g.dma_start(wvblk[par][:].rearrange("r (c e) -> r c e", c=DC),
                           ap),
               waits=[state["wv_read"][par]], incs=[inc(wvs[par], 16)])

    def qk_prod(b, wblk, wtok, oute, outo):
        for qc in range(QC):
            n0 = qc * CW
            nn = CW
            be, bo = (qc % 2) * 2, (qc % 2) * 2 + 1
            for eo, bank in ((0, be), (1, bo)):
                for c in range(DC):
                    P.emit("t", lambda t, eo=eo, bank=bank, c=c, n0=n0,
                           nn=nn, wblk=wblk: t.matmul(
                               ps[bank][:, 0:nn],
                               wblk[:, c * 256 + eo * 128:
                                    c * 256 + eo * 128 + 128],
                               xbT[:, c * S + n0:c * S + n0 + nn],
                               start=(c == 0), stop=(c == DC - 1)),
                           waits=([wtok, (xts[b], xts[b].n),
                                   bank_free[bank]] if c == 0 else ()),
                           incs=([inc(tq)] if c == DC - 1 else ()))
            mmtok = tq.tok()
            cS = cosb[:, n0:n0 + nn]
            sS = sinb[:, n0:n0 + nn]
            P.emit("v", lambda v, be=be, nn=nn, cS=cS:
                   v.tensor_tensor(rtmp[0][:, 0:nn], ps[be][:, 0:nn], cS,
                                   ALU.mult),
                   waits=[mmtok, (cs, cs.n), vq.tok()])
            P.emit("v", lambda v, bo=bo, nn=nn, sS=sS:
                   v.tensor_tensor(rtmp[1][:, 0:nn], ps[bo][:, 0:nn], sS,
                                   ALU.mult), incs=[inc(vq)])
            w1_ = vq.tok()
            P.emit("v", lambda v, nn=nn, n0=n0, oute=oute:
                   v.tensor_tensor(oute[:, n0:n0 + nn], rtmp[0][:, 0:nn],
                                   rtmp[1][:, 0:nn], ALU.subtract),
                   waits=[w1_], incs=[inc(vq)])
            subtok = vq.tok()
            P.emit("v", lambda v, be=be, nn=nn, sS=sS:
                   v.tensor_tensor(rtmp[0][:, 0:nn], ps[be][:, 0:nn], sS,
                                   ALU.mult), waits=[subtok])
            P.emit("v", lambda v, bo=bo, nn=nn, cS=cS:
                   v.tensor_tensor(rtmp[1][:, 0:nn], ps[bo][:, 0:nn], cS,
                                   ALU.mult), incs=[inc(vq)])
            w2_ = vq.tok()
            P.emit("v", lambda v, nn=nn, n0=n0, outo=outo:
                   v.tensor_tensor(outo[:, n0:n0 + nn], rtmp[0][:, 0:nn],
                                   rtmp[1][:, 0:nn], ALU.add),
                   waits=[w2_], incs=[inc(vq)])
            done = vq.tok()
            bank_free[be] = done
            bank_free[bo] = done

    def v_prod(b, par):
        wtok = (wvs[par], wvs[par].n)
        war = state["vpair_free"]
        for tc in range(TT):
            bank = 4 + tc % 2
            for c in range(DC):
                P.emit("t", lambda t, tc=tc, c=c, bank=bank, par=par:
                       t.matmul(
                           ps[bank][:, 0:256],
                           xbT[:, c * S + tc * 128:c * S + tc * 128 + 128],
                           wvblk[par][:, c * 256:(c + 1) * 256],
                           start=(c == 0), stop=(c == DC - 1)),
                       waits=([wtok, (xts[b], xts[b].n), bank_free[bank]]
                              if c == 0 else ()),
                       incs=([inc(tq)] if c == DC - 1 else ()))
            mmtok = tq.tok()
            P.emit("s", lambda s, tc=tc, bank=bank:
                   s.copy(vpair[:, tc * 256:(tc + 1) * 256],
                          ps[bank][:, 0:256]),
                   waits=[mmtok, war] if tc == 0 else [mmtok],
                   incs=[inc(sq)])
            bank_free[bank] = sq.tok()

    def emit_attention_pair(p, b, prefetch=None):
        P.label = f"attn_b{b}p{p}"
        par = p % 2
        wtokqk = (wqks[par], wqks[par].n)
        qk_prod(b, wqblk[par], wtokqk, qe, qo)
        qk_prod(b, wkblk[par], wtokqk, ke, ko)
        state["wqk_read"][par] = tq.tok()
        ropetok = vq.tok()
        v_prod(b, par)
        state["wv_read"][par] = tq.tok()
        vtok = sq.tok()
        if prefetch is not None:
            emit_pair_weight_loads(*prefetch)
        ptoks = {}
        rctoks = {}

        def emit_scores(qc, kc):
            q0 = qc * CW
            qn = CW
            k0 = kc * 128
            for hb in (0, 1):
                pb = 64 * hb
                sbank = hb * 2 + kc % 2
                P.emit("t", lambda t, sbank=sbank, pb=pb, k0=k0, q0=q0,
                       qn=qn: t.matmul(
                           ps[sbank][:, 0:qn],
                           ke[pb:pb + 64, k0:k0 + 128],
                           qe[pb:pb + 64, q0:q0 + qn],
                           start=True, stop=False),
                       waits=([bank_free[sbank], ropetok]
                              if (qc == 0 and kc == 0)
                              else [bank_free[sbank]]))
            for hb in (0, 1):
                pb = 64 * hb
                sbank = hb * 2 + kc % 2
                P.emit("t", lambda t, sbank=sbank, pb=pb, k0=k0, q0=q0,
                       qn=qn: t.matmul(
                           ps[sbank][:, 0:qn],
                           ko[pb:pb + 64, k0:k0 + 128],
                           qo[pb:pb + 64, q0:q0 + qn],
                           start=False, stop=True),
                       incs=[inc(tq)])
                smm = tq.tok()
                pi = hb * 2 + kc % 2
                pt = ptile[pi]
                P.emit("s", lambda s, pt=pt, sbank=sbank, qn=qn:
                       s.activation(pt[:, 0:qn], ps[sbank][:, 0:qn],
                                    AF.Exp, scale=ISQ128),
                       waits=[smm, state["ptile_free"][pi]],
                       incs=[inc(sq)])
                exptok = sq.tok()
                bank_free[sbank] = exptok
                ptok = exptok
                qoff = k0 - q0
                if qoff >= 0:
                    if qoff > 0:
                        P.emit("v", lambda v, pt=pt, qoff=qoff:
                               v.memset(pt[:, 0:qoff], 0.0),
                               waits=[exptok])
                    P.emit("v", lambda v, pt=pt, qoff=qoff:
                           v.tensor_tensor(
                               pt[:, qoff:qoff + 128],
                               pt[:, qoff:qoff + 128], trib[:],
                               ALU.mult),
                           waits=([exptok] if qoff == 0 else ()),
                           incs=[inc(vq)])
                    ptok = vq.tok()
                ptoks[(kc, hb)] = ptok

        def emit_sumpv(qc, kc):
            qn = CW
            kcmax = (qc + 1) * KPQ
            last = kc == kcmax - 1
            if last:
                for hb in (0, 1):
                    pi = hb * 2 + kc % 2
                    pt = ptile[pi]
                    rb = 6 + hb
                    P.emit("t", lambda t, rb=rb, pt=pt, qn=qn: t.matmul(
                        ps[rb][0:1, 0:qn], onesb[:, 0:1],
                        pt[:, 0:qn], start=False, stop=True),
                        waits=[ptoks[(kc, hb)]], incs=[inc(tq)])
                    rstok = tq.tok()
                    P.emit("s", lambda s, rb=rb, qn=qn, hb=hb: s.activation(
                        bcsb[hb][0:1, 0:qn], ps[rb][0:1, 0:qn], AF.Ln),
                        waits=[rstok, state["bcsb_free"][hb]],
                        incs=[inc(sq)])
                    P.emit("s", lambda s, qn=qn, hb=hb: s.activation(
                        bcsb[hb][0:1, 0:qn], bcsb[hb][0:1, 0:qn], AF.Exp,
                        scale=-1.0),
                        waits=[sq.tok()], incs=[inc(sq)])
                    rctoks[hb] = sq.tok()
                for hb in (0, 1):
                    pi = hb * 2 + kc % 2
                    pt = ptile[pi]
                    ob = 4 + hb
                    P.emit("t", lambda t, ob=ob, pt=pt, qn=qn, kc=kc,
                           hb=hb: t.matmul(
                               ps[ob][:, 0:qn],
                               vpair[:, kc * 256 + hb * 128:
                                     kc * 256 + hb * 128 + 128],
                               pt[:, 0:qn], start=False, stop=True),
                           incs=[inc(tq)])
                    state["ptile_free"][pi] = tq.tok()
                return
            for hb in (0, 1):
                pi = hb * 2 + kc % 2
                pt = ptile[pi]
                rb = 6 + hb
                ob = 4 + hb
                P.emit("t", lambda t, rb=rb, pt=pt, qn=qn, kc=kc:
                       t.matmul(
                           ps[rb][0:1, 0:qn], onesb[:, 0:1],
                           pt[:, 0:qn], start=(kc == 0), stop=False),
                       waits=([ptoks[(kc, hb)], bank_free[rb]] if kc == 0
                              else [ptoks[(kc, hb)]]))
                P.emit("t", lambda t, ob=ob, pt=pt, qn=qn, kc=kc, hb=hb:
                       t.matmul(
                           ps[ob][:, 0:qn],
                           vpair[:, kc * 256 + hb * 128:
                                 kc * 256 + hb * 128 + 128],
                           pt[:, 0:qn], start=(kc == 0), stop=False),
                       waits=([vtok, bank_free[ob]] if kc == 0 else ()),
                       incs=[inc(tq)])
                state["ptile_free"][pi] = tq.tok()

        def emit_cluster(qc):
            q0 = qc * CW
            qn = CW
            for hb in (0, 1):
                h = 2 * p + hb
                rb = 6 + hb
                ob = 4 + hb
                ost = otstage[hb]
                P.emit("t", lambda t, rb=rb, qn=qn, hb=hb: t.matmul(
                    ps[rb][:, 0:qn], onesf[:], bcsb[hb][0:1, 0:qn],
                    start=True, stop=True),
                    waits=[rctoks[hb]], incs=[inc(tq)])
                bctok = tq.tok()
                P.emit("s", lambda s, rb=rb, qn=qn, hb=hb: s.copy(
                    bcsb[hb][:, 0:qn], ps[rb][:, 0:qn]),
                    waits=[bctok], incs=[inc(sq)])
                bank_free[rb] = sq.tok()
                bcev = sq.tok()
                P.emit("v", lambda v, ob=ob, q0=q0, qn=qn, ost=ost,
                       hb=hb: v.tensor_tensor(
                           ost[:, q0:q0 + qn], ps[ob][:, 0:qn],
                           bcsb[hb][:, 0:qn], ALU.mult),
                       waits=[bcev] + ([state["otstage_free"][hb]]
                                       if qc == 0 else []),
                       incs=[inc(vq)])
                bank_free[ob] = vq.tok()
                state["bcsb_free"][hb] = vq.tok()
                if qc == QC - 1:
                    oev = vq.tok()
                    P.emit("y", lambda y, h=h, ost=ost, b=b: y.dma_start(
                        d_oT[b, h], ost[:]), waits=[oev],
                        incs=[inc(oto[hb], 16)])
                    state["otstage_free"][hb] = (oto[hb], oto[hb].n)

        tiles = [(qc, kc) for qc in range(QC)
                 for kc in range((qc + 1) * KPQ)]
        prev = None
        pend_cl = None
        for (qc, kc) in tiles:
            emit_scores(qc, kc)
            if prev is not None:
                if pend_cl is not None:
                    emit_cluster(pend_cl)
                    pend_cl = None
                emit_sumpv(*prev)
                if prev[0] != qc:
                    pend_cl = prev[0]
            prev = (qc, kc)
        emit_sumpv(*prev)
        emit_cluster(QC - 1)
        state["qk_free"] = tq.tok()
        state["vpair_free"] = tq.tok()

    def emit_wo_load():
        qtoks = []
        for hq in range(4):
            nh4 = DC // 4
            ap = d_wo.rearrange("(c r) e -> r c e", r=128)[
                :, hq * nh4:(hq + 1) * nh4, :]
            P.emit("g", lambda g, ap=ap, hq=hq, nh4=nh4: g.dma_start(
                wobuf[:, hq * nh4 * D:(hq + 1) * nh4 * D].rearrange(
                    "r (c e) -> r c e", c=nh4), ap),
                waits=([state["wobuf_free"]] if hq == 0 else ()),
                incs=[inc(wos, 16)])
            qtoks.append((wos, wos.n))
        return qtoks

    def emit_ln(zview, sqb, outf_list, ztok, par, extra_war):
        col = lncol[:, par * 4:par * 4 + 4]
        P.emit("v", lambda v: v.reduce_sum(col[:, 0:1], zview, axis=AX.X),
               waits=[ztok, state["lncol_free"][par]], incs=[inc(vq)])
        t1 = vq.tok()
        P.emit("v", lambda v: v.tensor_scalar(
            col[:, 1:2], col[:, 0:1], 1.0 / D, None, ALU.mult),
            waits=[t1], incs=[inc(vq)])
        t2 = vq.tok()
        P.emit("v", lambda v: v.tensor_scalar(
            zview, zview, col[:, 1:2], None, ALU.subtract),
            waits=[t2], incs=[inc(vq)])
        t3 = vq.tok()
        P.emit("s", lambda s: s.activation(sqb[:], zview, AF.Square),
               waits=[t3, state["sqf_free"]], incs=[inc(sq)])
        t4 = sq.tok()
        P.emit("v", lambda v: v.reduce_sum(col[:, 2:3], sqb[:], axis=AX.X),
               waits=[t4], incs=[inc(vq)])
        t5 = vq.tok()
        state["sqf_free"] = t5
        P.emit("s", lambda s: s.activation(
            col[:, 3:4], col[:, 2:3], AF.Ln, scale=CVALS["invD"],
            bias=CVALS["eps"]), waits=[t5], incs=[inc(sq)])
        t6 = sq.tok()
        P.emit("s", lambda s: s.activation(
            col[:, 3:4], col[:, 3:4], AF.Exp, scale=CVALS["nhalf"]),
            waits=[t6], incs=[inc(sq)])
        t7 = sq.tok()
        for outf, war in zip(outf_list, extra_war):
            P.emit("v", lambda v, outf=outf: v.tensor_scalar(
                outf[:], zview, col[:, 3:4], None, ALU.mult),
                waits=[t7] + list(war), incs=[inc(vq)])
        state["lncol_free"][par] = vq.tok()

    pend_tr = []

    def emit_transposes(b, tt, par, lntok, sptok):
        for c in range(DC):
            tb = 4 + c % 4
            P.emit("t", lambda t, c=c, par=par, tb=tb: t.transpose(
                ps[tb][:, 0:128], y1f[par][:, c * 128:(c + 1) * 128],
                identb[:]),
                waits=([lntok, bank_free[tb], (cs, cs.n)] if c < 4
                       else [bank_free[tb]]),
                incs=[inc(tq)])
            trtok = tq.tok()
            if c % 2 == 0:
                P.emit("s", lambda s, c=c, par=par, tb=tb: s.copy(
                    y1Tst[par][:, c * 128:(c + 1) * 128], ps[tb][:, 0:128]),
                    waits=([trtok, state["y1Tst_free"][par]] if c < 2
                           else [trtok]),
                    incs=[inc(sq)])
                bank_free[tb] = sq.tok()
            else:
                P.emit("v", lambda v, c=c, par=par, tb=tb: v.tensor_copy(
                    y1Tst[par][:, c * 128:(c + 1) * 128], ps[tb][:, 0:128]),
                    waits=([trtok, state["y1Tst_free"][par]] if c < 2
                           else [trtok]),
                    incs=[inc(vq)])
                bank_free[tb] = vq.tok()
        trev = [sq.tok(), vq.tok()]
        state["y1f_free"][par] = trev
        dst = d_y1T[:, :, b * S + tt * 128: b * S + (tt + 1) * 128]
        P.emit("g", lambda g, dst=dst, par=par: g.dma_start(
            dst.rearrange("c r e -> r c e"),
            y1Tst[par][:].rearrange("r (c e) -> r c e", c=DC)),
            waits=[trev, sptok], incs=[inc(y1to[par], 16)])
        state["y1Tst_free"][par] = (y1to[par], y1to[par].n)

    def emit_w_in_dma(b, tt):
        par = tt % 2
        P.emit("y", lambda y, b=b, tt=tt, par=par: y.dma_start(
            xnat[par][:], d_xn[b, tt * 128:(tt + 1) * 128, :]),
            waits=[state["xnat_free"][par]], incs=[inc(xns[par], 16)])
        otap = d_oT[b, :, :, tt * 128:(tt + 1) * 128]
        P.emit("y", lambda y, otap=otap, par=par: y.dma_start(
            ottile[par][:].rearrange("r (h e) -> r h e", h=NH),
            otap.rearrange("h r e -> r h e")),
            waits=[state["ottile_free"][par],
                   (oto[0], oto[0].n), (oto[1], oto[1].n)],
            incs=[inc(otis[par], 16)])
        return ((xns[par], xns[par].n), (otis[par], otis[par].n))

    def emit_w_phase(b):
        P.label = f"W{b}"
        woqtoks = emit_wo_load()
        ndc = D // 512
        toks = emit_w_in_dma(b, 0)
        for tt in range(TT):
            par = tt % 2
            P.label = f"W{b}t{tt}"
            xtok, ottok = toks
            for h in range(NH):
                for dc in range(ndc):
                    P.emit("t", lambda t, h=h, dc=dc, par=par: t.matmul(
                        ps[dc][:, :],
                        ottile[par][:, h * 128:(h + 1) * 128],
                        wobuf[:, h * D + dc * 512: h * D + dc * 512 + 512],
                        start=(h == 0), stop=(h == NH - 1)),
                        waits=(([ottok] +
                                [bank_free[i] for i in range(ndc)])
                               if (h == 0 and dc == 0) else []) +
                        ([woqtoks[h // (NH // 4)]]
                         if (h % (NH // 4) == 0 and dc == 0) else []),
                        incs=([inc(tq)]
                              if (h == NH - 1 and dc == ndc - 1) else ()))
            wtok = tq.tok()
            state["ottile_free"][par] = wtok
            if tt + 1 < TT:
                toks = emit_w_in_dma(b, tt + 1)
            if pend_tr:
                emit_transposes(*pend_tr.pop(0))
            for dc in range(ndc):
                P.emit("v", lambda v, dc=dc, par=par: v.tensor_tensor(
                    zsb[:, dc * 512:(dc + 1) * 512], ps[dc][:, :],
                    xnat[par][:, dc * 512:(dc + 1) * 512], ALU.add),
                    waits=([wtok, xtok, state["zsb_free"]] if dc == 0
                           else []),
                    incs=([inc(vq)] if dc == ndc - 1 else ()))
            ztok = vq.tok()
            for dc in range(ndc):
                bank_free[dc] = ztok
            state["xnat_free"][par] = ztok
            emit_ln(zsb[:], sqf, [y1f[par]], ztok, par,
                    [[state["y1f_free"][par], state["y1fspill"][par]]])
            lntok = vq.tok()
            state["zsb_free"] = lntok
            P.emit("g", lambda g, b=b, tt=tt, par=par: g.dma_start(
                d_y1f[b * S + tt * 128: b * S + (tt + 1) * 128, :],
                y1f[par][:]),
                waits=[lntok], incs=[inc(y1fo[par], 16)])
            sptok = (y1fo[par], y1fo[par].n)
            state["y1fspill"][par] = sptok
            pend_tr.append((b, tt, par, lntok, sptok))
        while pend_tr:
            emit_transposes(*pend_tr.pop(0))
        state["wobuf_free"] = tq.tok()

    def emit_u_phase():
        P.label = "U"
        uw = [(y1to[0], y1to[0].n), (y1to[1], y1to[1].n),
              state["xbT_free"], state["wobuf_free"]]
        for c in range(DC):
            P.emit("y" if c % 2 == 0 else "g",
                   lambda e, c=c: e.dma_start(
                y1T[:, c * B * S:(c + 1) * B * S], d_y1T[c]),
                waits=(uw if c < 2 else ()), incs=[inc(y1ti, 16)])
        y1ttok = (y1ti, y1ti.n)
        ntc = B * S // 512
        pend_h = []

        def emit_h128(hc, tc, multok):
            tb = 4 + tc % 2
            hb_ = hidb[tc % 2]
            P.emit("t", lambda t, hb_=hb_, tb=tb: t.matmul(
                ps[tb][:, :], h128b[:], hb_[:], start=True, stop=True),
                waits=[multok, bank_free[tb]], incs=[inc(tq)])
            htok = tq.tok()
            state["hidb_free"][tc % 2] = htok
            h2 = hid2b[tc % 2]
            P.emit("s", lambda s, h2=h2, tb=tb: s.copy(
                h2[:], ps[tb][:, :]),
                waits=[htok, state["hid2_free"][tc % 2]],
                incs=[inc(sq)])
            bank_free[tb] = sq.tok()
            h2ev = sq.tok()
            dsth = d_hid[:, hc, tc * 512:(tc + 1) * 512]
            P.emit("y", lambda y, dsth=dsth, h2=h2: y.dma_start(
                dsth, h2[:]),
                waits=[h2ev], incs=[inc(hido[tc % 2], 16)])
            state["hid2_free"][tc % 2] = (hido[tc % 2],
                                          hido[tc % 2].n)

        for hc in range(A):
            par = hc % 2
            for (dst, src, war) in ((w1blk[par], d_w1,
                                     state["w13_free"][par]),
                                    (w3blk[par], d_w3, ZT)):
                ap = src.rearrange("(c r) e -> r c e", r=128)[
                    :, :, hc * 128:(hc + 1) * 128]
                P.emit("g", lambda g, dst=dst, ap=ap: g.dma_start(
                    dst[:].rearrange("r (c e) -> r c e", c=DC), ap),
                       waits=[war], incs=[inc(w13s[par], 16)])
            wtok = (w13s[par], w13s[par].n)
            for tc in range(ntc):
                gb = tc % 2
                lb = 2 + tc % 2
                for c in range(DC):
                    P.emit("t", lambda t, c=c, gb=gb, par=par, tc=tc:
                           t.matmul(ps[gb][:, :],
                                    w1blk[par][:, c * 128:(c + 1) * 128],
                                    y1T[:, c * B * S + tc * 512:
                                        c * B * S + tc * 512 + 512],
                                    start=(c == 0), stop=(c == DC - 1)),
                           waits=([wtok, y1ttok, bank_free[gb]]
                                  if c == 0 else ()))
                if pend_h:
                    emit_h128(*pend_h.pop(0))
                for c in range(DC):
                    P.emit("t", lambda t, c=c, lb=lb, par=par, tc=tc:
                           t.matmul(ps[lb][:, :],
                                    w3blk[par][:, c * 128:(c + 1) * 128],
                                    y1T[:, c * B * S + tc * 512:
                                        c * B * S + tc * 512 + 512],
                                    start=(c == 0), stop=(c == DC - 1)),
                           waits=([bank_free[lb]] if c == 0 else ()),
                           incs=([inc(tq)] if c == DC - 1 else ()))
                mmtok = tq.tok()
                sb_ = silub[tc % 2]
                hb_ = hidb[tc % 2]
                P.emit("s", lambda s, gb=gb, sb_=sb_: s.activation(
                    sb_[:], ps[gb][:, :], AF.Silu),
                    waits=[mmtok, state["silu_free"][tc % 2]],
                    incs=[inc(sq)])
                silutok = sq.tok()
                bank_free[gb] = silutok
                P.emit("v", lambda v, lb=lb, sb_=sb_, hb_=hb_:
                       v.tensor_tensor(hb_[:], sb_[:], ps[lb][:, :],
                                       ALU.mult),
                       waits=[silutok, state["hidb_free"][tc % 2]],
                       incs=[inc(vq)])
                multok = vq.tok()
                bank_free[lb] = multok
                state["silu_free"][tc % 2] = multok
                pend_h.append((hc, tc, multok))
            state["w13_free"][par] = tq.tok()
        while pend_h:
            emit_h128(*pend_h.pop(0))

    def emit_d_phase():
        P.label = "D"
        NSUPS = B * TT // NSUP
        hid_all = [(hido[0], hido[0].n), (hido[1], hido[1].n)]
        pm_all = [(pmos[0], pmos[0].n), (pmos[1], pmos[1].n)]
        y1f_all = [(y1fo[0], y1fo[0].n), (y1fo[1], y1fo[1].n)]
        ndc4 = D // 512
        nblk = A // W2B
        for sup in range(NSUPS):
            spar = sup % 2
            for half in range(2):
                a0 = half * (A // 2)
                src = d_hid[:, a0:a0 + A // 2,
                            sup * NSUP * 128:(sup + 1) * NSUP * 128]
                P.emit("g" if half == 0 else "y",
                       lambda e, src=src, a0=a0: e.dma_start(
                           hsup[:, a0 * 512:(a0 + A // 2) * 512].rearrange(
                               "r (a e) -> r a e", a=A // 2), src),
                       waits=(hid_all + [state["hsup_free"]]),
                       incs=[inc(hidi[spar], 16)])
            for j in range(NSUP):
                tt = sup * NSUP + j
                P.emit("y", lambda y, tt=tt, j=j: y.dma_start(
                    y1in[j][:], d_y1f[tt * 128:(tt + 1) * 128, :]),
                    waits=[state["y1in_free"][j]] + (y1f_all
                                                     if sup == 0 else []),
                    incs=[inc(y1is[j], 16)])
            htok = (hidi[spar], hidi[spar].n)
            ytoks = [(y1is[j], y1is[j].n) for j in range(NSUP)]
            for dc4 in range(ndc4):
                for bi in range(nblk):
                    wb = bi % 4
                    ap = d_w2pp[dc4, bi // 2, :,
                                (bi % 2) * W2B * 512:
                                (bi % 2 + 1) * W2B * 512]
                    P.emit(("y", "g", "s", "y")[bi % 4],
                           lambda e, ap=ap, wb=wb: e.dma_start(
                        w2blk[wb][:], ap),
                        waits=[state["w2blk_free"][wb]] +
                        (pm_all if sup == 0 and dc4 == 0 else []),
                        incs=[inc(w2ps[wb], 16)])
                    wtok = (w2ps[wb], w2ps[wb].n)
                    for hoff in range(W2B):
                        hc = bi * W2B + hoff
                        for j in range(NSUP):
                            P.emit("t", lambda t, hc=hc, j=j, hoff=hoff,
                                   wb=wb: t.matmul(
                                       ps[j][:, :],
                                       hsup[:, hc * 512 + j * 128:
                                            hc * 512 + j * 128 + 128],
                                       w2blk[wb][:, hoff * 512:
                                                 hoff * 512 + 512],
                                       start=(hc == 0), stop=(hc == A - 1)),
                                   waits=([htok, wtok] +
                                          [bank_free[i]
                                           for i in range(NSUP)])
                                   if hc == 0 and j == 0 else
                                   ([wtok] if hoff == 0 and j == 0
                                    else ()),
                                   incs=([inc(tq)]
                                         if (hoff == W2B - 1
                                             and j == NSUP - 1)
                                         else ()))
                    state["w2blk_free"][wb] = tq.tok()
                mmtok = tq.tok()
                for j in range(NSUP):
                    tt = sup * NSUP + j
                    P.emit("v", lambda v, j=j, dc4=dc4:
                           v.tensor_tensor(
                               z2sb[:, j * D + dc4 * 512:
                                    j * D + dc4 * 512 + 512],
                               ps[j][:, :],
                               y1in[j][:, dc4 * 512:(dc4 + 1) * 512],
                               ALU.add),
                           waits=[mmtok, ytoks[j]] +
                           ([state["z2_free"]] if j == 0 and dc4 == 0
                            else []),
                           incs=[inc(vq)])
                    bank_free[j] = vq.tok()
            z2tok = vq.tok()
            for j in range(NSUP):
                state["y1in_free"][j] = z2tok
                tt = sup * NSUP + j
                b = tt // TT
                trow = tt % TT
                par = tt % 2
                zview = z2sb[:, j * D:(j + 1) * D]
                emit_ln(zview, sqfD, [outsb[par]], z2tok, par,
                        [[state["outsb_free"][par]]])
                otok = vq.tok()
                P.emit("y", lambda y, b=b, trow=trow, par=par: y.dma_start(
                    d_y[b, trow * 128:(trow + 1) * 128, :], outsb[par][:]),
                    waits=[otok], incs=[inc(outs[par], 16)])
                state["outsb_free"][par] = (outs[par], outs[par].n)
            state["z2_free"] = vq.tok()
            state["hsup_free"] = tq.tok()

    emit_consts()
    pm_per_pair = max(1, (n_fold + B * PAIRS - 1) // (B * PAIRS))
    pm_next = [0]
    for b in range(B):
        emit_pair_weight_loads(0, b)
        emit_xt_load(b)
        for p in range(PAIRS):
            for _ in range(pm_per_pair):
                if pm_next[0] < n_fold:
                    emit_fold_chunk(pm_next[0])
                    pm_next[0] += 1
            nxt = (p + 1, b) if p + 1 < PAIRS else None
            emit_attention_pair(p, b, nxt)
        while b == B - 1 and pm_next[0] < n_fold:
            emit_fold_chunk(pm_next[0])
            pm_next[0] += 1
        phase_barrier()
        emit_w_phase(b)
        state["xbT_free"] = tq.tok()
        if b < B - 1:
            phase_barrier()
    phase_barrier()
    emit_u_phase()
    phase_barrier()
    emit_d_phase()
    fin = [(outs[0], outs[0].n), (outs[1], outs[1].n)]
    P.emit("y", lambda y: None, waits=fin)
    P.run()
    return nc


def _hadamard(n):
    H = np.array([[1.0]], dtype=np.float32)
    while H.shape[0] < n:
        H = np.block([[H, H], [H, -H]])
    return H


def host_consts(cfg: Cfg):
    S, A = cfg.S, cfg.A
    inv_freq = 1.0 / (10000.0 ** (np.arange(0, 128, 2, dtype=np.float32)
                                  / 128.0))
    t = np.arange(S, dtype=np.float32)
    fr = np.outer(inv_freq, t)
    cos = np.cos(fr).astype(np.float32)
    sin = np.sin(fr).astype(np.float32)
    cosd = np.concatenate([cos, cos], axis=0)
    sind = np.concatenate([sin, sin], axis=0)
    h128c = (_hadamard(128) * (2.0 ** -3.5)).astype(np.float32)
    AA = min(A, 64)
    hA = (_hadamard(AA) * (2.0 ** -3)).astype(np.float32)
    hAc = np.zeros((128, AA), dtype=np.float32)
    hAc[0:AA] = hA
    if A >= 64:
        hAc[64:128] = hA
    identc = np.eye(128, dtype=np.float32)
    tric = np.triu(np.ones((128, 128), dtype=np.float32))
    return dict(cosd=cosd, sind=sind, h128c=h128c, hAc=hAc, identc=identc,
                tric=tric)


def qk_perm(cfg: Cfg):
    perm = np.zeros(cfg.D, dtype=np.int64)
    for p in range(cfg.PAIRS):
        for eo in range(2):
            ci = 2 * p + eo
            for hb in range(2):
                h = 2 * p + hb
                for i in range(64):
                    perm[ci * 128 + hb * 64 + i] = h * 128 + 2 * i + eo
    return perm


def kernel(L0_attn_x, L0_attn_Wq, L0_attn_Wk, L0_attn_Wv, L0_attn_Wo,
           L0_ffn_W1, L0_ffn_W2, L0_ffn_W3):
    cfg = Cfg()
    x = np.ascontiguousarray(np.asarray(L0_attn_x, dtype=np.float32))
    Wq = np.asarray(L0_attn_Wq, np.float32)
    Wk = np.asarray(L0_attn_Wk, np.float32)
    Wv = np.asarray(L0_attn_Wv, np.float32)
    Wo = np.asarray(L0_attn_Wo, np.float32)
    W1 = np.asarray(L0_ffn_W1, np.float32)
    W2 = np.asarray(L0_ffn_W2, np.float32)
    W3 = np.asarray(L0_ffn_W3, np.float32)

    qperm = qk_perm(cfg)
    shared = host_consts(cfg)
    shared["wqT_p"] = np.ascontiguousarray(Wq.T[:, qperm])
    shared["wkT_p"] = np.ascontiguousarray(Wk.T[:, qperm])
    shared["wvT"] = np.ascontiguousarray(Wv.T)
    shared["woT"] = np.ascontiguousarray(Wo.T)
    shared["w1T_p"] = np.ascontiguousarray(W1.T)
    shared["w3T_p"] = np.ascontiguousarray(W3.T)
    shared["w2T"] = np.ascontiguousarray(W2.T)

    nc = build(cfg)
    in_maps = []
    for c in range(cfg.NCORES):
        d = dict(shared)
        xs = x[c * cfg.B:(c + 1) * cfg.B]
        d["xT"] = np.ascontiguousarray(xs.transpose(0, 2, 1))
        d["x_nat"] = np.ascontiguousarray(xs)
        in_maps.append(d)
    res = run_bass_kernel_spmd(nc, in_maps, core_ids=list(range(cfg.NCORES)))
    out = np.concatenate([np.asarray(r["y"], np.float32)
                          for r in res.results], axis=0)
    return out
